# version 59 (final)
# baseline (speedup 1.0000x reference)
import contextlib
import numpy as np
import concourse.bass as bass
import concourse.mybir as mybir
from concourse.bass_utils import run_bass_kernel_spmd

F32 = mybir.dt.float32
BF16 = mybir.dt.bfloat16
AF = mybir.ActivationFunctionType
OP = mybir.AluOpType

D = 1024
NT = 8
TL = 2050
CW = 2052
EPS = 1e-5

OWNER = {"t": "tensor", "v": "vector", "s": "scalar", "g": "gpsimd"}


class Prog:

    def __init__(self, nc, sems):
        self.nc = nc
        self.sems = sems
        self.cnt = {k: 0 for k in sems}
        self.items = []

    def emit(self, eng, fn, waits=(), inc=None, amt=None):
        w = []
        for tk in waits:
            if tk is None:
                continue
            if isinstance(tk, list):
                w.extend(t for t in tk if t is not None)
            else:
                w.append(tk)
        for _t in w:
            assert isinstance(_t, tuple) and len(_t) == 2, f"bad token {_t}"
        tok = None
        if inc is not None:
            if amt is None:
                amt = 16 if inc.startswith("d") else 1
            self.cnt[inc] += amt
            tok = (inc, self.cnt[inc])
        self.items.append((eng, fn, tuple(w), inc, amt))
        return tok

    def barrier(self):
        toks = []
        dw = [(k, v) for k, v in self.cnt.items()
              if k.startswith("d") and v > 0]
        toks.append(self.emit("tensor", lambda e: e.drain(fusable=False),
                              waits=dw, inc="t"))
        toks.append(self.emit("vector", lambda e: e.drain(fusable=False),
                              waits=dw, inc="v"))
        toks.append(self.emit("scalar", lambda e: e.drain(fusable=False),
                              waits=dw, inc="s"))
        toks.extend(dw)
        return toks

    def flush(self, block):
        per = {}
        for it in self.items:
            per.setdefault(it[0], []).append(it)
        for engname in ("sync", "gpsimd", "vector", "scalar", "tensor"):
            items = per.get(engname)
            if not items:
                continue

            def body(sec, items=items, engname=engname):
                water = {}
                own = {}
                pending_inc = False
                for (_eng, fn, waits, inc, amt) in items:
                    need = {}
                    for (snm, v) in waits:
                        if water.get(snm, 0) >= v:
                            continue
                        if need.get(snm, 0) < v:
                            need[snm] = v
                    if inc is not None and pending_inc and not need:
                        sec.wait_ge(self.sems[inc], own.get(inc, 0))
                        pending_inc = False
                    for snm, v in need.items():
                        sec.wait_ge(self.sems[snm], v)
                        water[snm] = v
                        pending_inc = False
                    r = fn(sec)
                    if inc is not None:
                        if r is None:
                            r = sec.nop(nofuse=True)
                        r.then_inc(self.sems[inc], amt)
                        own[inc] = own.get(inc, 0) + amt
                        pending_inc = True
            getattr(block, engname)(body)
        self.items = []


def _cover(toks, lo, hi):
    return [t[2] for t in toks if t is not None and lo < t[1] and hi > t[0]]


def _chunks(a, b, step=512):
    out = []
    lo = a
    while lo < b:
        hi = min(lo + step, b)
        out.append((lo, hi))
        lo = hi
    return out


def build_nc():
    nc = bass.Bass("TRN2", target_bir_lowering=False)

    xs = nc.dram_tensor("xs", [TL, D], F32, kind="ExternalInput").ap()
    Wq_h = nc.dram_tensor("Wq", [D, D], F32, kind="ExternalInput").ap()
    Wk_h = nc.dram_tensor("Wk", [D, D], F32, kind="ExternalInput").ap()
    Wv_h = nc.dram_tensor("Wv", [D, D], F32, kind="ExternalInput").ap()
    bq_h = nc.dram_tensor("bq", [1, D], F32, kind="ExternalInput").ap()
    bk_h = nc.dram_tensor("bk", [1, D], F32, kind="ExternalInput").ap()
    bv_h = nc.dram_tensor("bv", [1, D], F32, kind="ExternalInput").ap()
    Wr_h = nc.dram_tensor("Wr", [D, D], F32, kind="ExternalInput").ap()
    Wkf_h = nc.dram_tensor("Wkf", [D, D], F32, kind="ExternalInput").ap()
    Wvf_h = nc.dram_tensor("Wvf", [D, D], F32, kind="ExternalInput").ap()
    mur_h = nc.dram_tensor("mur", [1, D], F32, kind="ExternalInput").ap()
    muk_h = nc.dram_tensor("muk", [1, D], F32, kind="ExternalInput").ap()
    mask_h = nc.dram_tensor("maskc", [128, 1], F32, kind="ExternalInput").ap()
    ident_h = nc.dram_tensor("identin", [128, 128], F32, kind="ExternalInput").ap()

    y_out = nc.dram_tensor("y", [2048, D], F32, kind="ExternalOutput").ap()
    import os as _os
    KDBG = bool(_os.environ.get("KDBG"))
    if KDBG:
        dbg_p0 = nc.dram_tensor("dbg_p0", [128, CW], F32, kind="ExternalOutput").ap()
        dbg_f1 = nc.dram_tensor("dbg_f1", [128, CW], F32, kind="ExternalOutput").ap()
        dbg_at = nc.dram_tensor("dbg_at", [128, CW], F32, kind="ExternalOutput").ap()
        dbg_s = nc.dram_tensor("dbg_s", [1, D], F32, kind="ExternalOutput").ap()
        dbg_wq = nc.dram_tensor("dbg_wq", [3 * 128, NT * D], mybir.dt.bfloat16,
                                kind="ExternalOutput").ap()
        dbg_bc = nc.dram_tensor("dbg_bc", [2 * 128, CW], F32,
                                kind="ExternalOutput").ap()
        dbg_cq = nc.dram_tensor("dbg_cq", [128, 24], F32,
                                kind="ExternalOutput").ap()

    Cdd = nc.dram_tensor("Cdd", [D, 2 * D], BF16)
    sdd = nc.dram_tensor("sdd", [1, 2 * D], BF16)
    wspill = nc.dram_tensor("wspill", [3 * 128, NT * D], BF16)
    wattn = nc.dram_tensor("wattn", [3 * 128, NT * D], BF16)
    brow_sc = nc.dram_tensor("brow_sc", [3, D], BF16)
    cc_in = nc.dram_tensor("cc_sin", [128, 128], F32)
    cc_out = nc.dram_tensor("cc_sout", [128, 128], F32)

    est = contextlib.ExitStack()
    with est:
        sem_names = ["dms", "dx0", "dx1", "dx2", "dx3", "dw0", "dw1",
                     "dwv0", "dwv1", "dpk0", "dpk1", "drow", "drw2",
                     "dab", "dbias", "dwA", "dwA2", "dwA3", "dspill",
                     "dsp2", "dsp3", "dwS0", "dwS1",
                     "dC0", "dC1", "dC2", "dC3", "dC4", "dC5", "dC6",
                     "dC7", "drot0", "drot1", "dsdd", "dS", "dgs", "drl",
                     "dout", "ddbg", "cc", "t", "v", "s", "g"]
        sem_names += [f"dnw{tg}{ci}" for tg in "12" for ci in range(10)]
        sems = {n: est.enter_context(nc.semaphore(n))
                for n in sem_names}
        P = Prog(nc, sems)

        XR = est.enter_context(nc.sbuf_tensor("XR", [128, NT * CW], F32))
        WB = [est.enter_context(nc.sbuf_tensor(f"WB{i}", [128, NT * D], BF16))
              for i in range(3)]
        AB_r = est.enter_context(nc.sbuf_tensor("ABr", [2, D], BF16))
        AB_k = est.enter_context(nc.sbuf_tensor("ABk", [2, D], BF16))
        identf = est.enter_context(nc.sbuf_tensor("identf", [128, 128], F32))
        identb = est.enter_context(nc.sbuf_tensor("identb", [128, 128], BF16))
        ones_row_b = est.enter_context(nc.sbuf_tensor("onesrb", [1, 128], BF16))
        negones_row_b = est.enter_context(nc.sbuf_tensor("nonesrb", [1, 128], BF16))
        ones_row_f = est.enter_context(nc.sbuf_tensor("onesrf", [1, 128], F32))
        ones_col_f = est.enter_context(nc.sbuf_tensor("onescf", [128, 1], F32))
        ones_col_b = est.enter_context(nc.sbuf_tensor("onescb", [128, 1], BF16))
        invD_col_b = est.enter_context(nc.sbuf_tensor("invDcb", [128, 1], BF16))
        mask_col = est.enter_context(nc.sbuf_tensor("maskcol", [128, 1], F32))
        mu_cols = est.enter_context(nc.sbuf_tensor("mucols", [128, 16], BF16))

        c_mask = P.emit("sync", lambda e: e.dma_start(out=mask_col[:], in_=mask_h),
                        inc="dms")
        t_id0 = P.emit("sync", lambda e: e.dma_start(out=identf[:], in_=ident_h),
                       inc="dms")
        c_mask = t_id0
        t_idb = P.emit("vector", lambda e: e.tensor_copy(out=identb[:], in_=identf[:]),
                       waits=[t_id0], inc="v")
        P.emit("vector", lambda e: e.memset(ones_row_b[:], 1.0))
        P.emit("vector", lambda e: e.memset(negones_row_b[:], -1.0))
        P.emit("vector", lambda e: e.memset(ones_row_f[:], 1.0))
        P.emit("vector", lambda e: e.memset(ones_col_f[:], 1.0))
        P.emit("vector", lambda e: e.memset(ones_col_b[:], 1.0))
        cst_eps = est.enter_context(nc.sbuf_tensor("cst_eps", [128, 1], F32))
        cst_nhalf = est.enter_context(nc.sbuf_tensor("cst_nhalf", [128, 1], F32))
        cst_half = est.enter_context(nc.sbuf_tensor("cst_half", [128, 1], F32))
        cst_neg1 = est.enter_context(nc.sbuf_tensor("cst_neg1", [128, 1], F32))
        P.emit("vector", lambda e: e.memset(cst_eps[:], float(EPS)))
        P.emit("vector", lambda e: e.memset(cst_nhalf[:], -0.5))
        P.emit("vector", lambda e: e.memset(cst_half[:], 0.5))
        P.emit("vector", lambda e: e.memset(cst_neg1[:], -1.0))
        nc.const_aps.register(F32, float(EPS), cst_eps[:])
        nc.const_aps.register(F32, -0.5, cst_nhalf[:])
        nc.const_aps.register(F32, 0.5, cst_half[:])
        nc.const_aps.register(F32, -1.0, cst_neg1[:])
        t_consts = P.emit("vector", lambda e: e.memset(invD_col_b[:], 1.0 / D),
                          inc="v")

        state = {}
        park = est.enter_context(nc.sbuf_tensor("park", [128, 2 * D], BF16))

        pst = contextlib.ExitStack()
        with pst:
            stage = pst.enter_context(nc.sbuf_tensor("stage", [128, NT * D], F32))
            wnat0 = pst.enter_context(nc.sbuf_tensor("wnat0", [128, NT * D], BF16))
            wnat1 = pst.enter_context(nc.sbuf_tensor("wnat1", [128, NT * D], BF16))
            wnat2 = pst.enter_context(nc.sbuf_tensor("wnat2", [128, NT * D], BF16))
            prow = pst.enter_context(nc.sbuf_tensor("prow", [1, D], BF16))
            prow2 = pst.enter_context(nc.sbuf_tensor("prow2", [1, D], BF16))
            ps_tr0 = pst.enter_context(nc.psum_tensor("pstr0", [128, 512], F32))
            ps_tr1 = pst.enter_context(nc.psum_tensor("pstr1", [128, 512], F32))
            ps_wtr0 = pst.enter_context(nc.psum_tensor("pswt0", [128, 512], BF16))
            ps_wtr1 = pst.enter_context(nc.psum_tensor("pswt1", [128, 512], BF16))
            ps_muc = pst.enter_context(nc.psum_tensor("psmuc", [128, 32], BF16))
            ps_abA = pst.enter_context(nc.psum_tensor("psabA", [1, 512], F32))
            ps_abB = pst.enter_context(nc.psum_tensor("psabB", [1, 512], F32))
            rowB = prow2
            def row_cast(src_h, dstrow, guard):
                return P.emit("gpsimd", lambda e, src_h=src_h, dstrow=dstrow:
                              e.dma_start(out=dstrow[:], in_=src_h),
                              waits=list(guard), inc="drow")

            cw_mur = row_cast(mur_h, prow, [])
            cw_muk = row_cast(muk_h, prow2, [])
            for half16 in range(2):
                rowt = prow if half16 == 0 else prow2
                cw = cw_mur if half16 == 0 else cw_muk
                for j8 in range(8):
                    j = half16 * 8 + j8

                    def ft(e, j=j, j8=j8, rowt=rowt):
                        return e.transpose(
                            out=ps_muc[:, 2 * j:2 * j + 1],
                            in_=rowt[0:1, j8 * 128:j8 * 128 + 128],
                            identity=identb[0:1, 0:1])
                    mu_tk = P.emit("tensor", ft, waits=[cw, t_idb],
                                   inc="t" if j8 == 7 else None)
            muc_ev = P.emit("vector", lambda e: e.tensor_copy(
                out=mu_cols[:], in_=ps_muc[:, 0:32:2]), waits=[mu_tk], inc="v")

            ps_tr = [ps_tr0, ps_tr1]
            ld = {}
            grp_ev = {}
            bank_guard = {}
            bidx = 0
            for g in range(4):
                for k in range(4):
                    i = g * 4 + k
                    slot = i % 8
                    w = grp_ev.get(g - 2, [])

                    def f(e, i=i, slot=slot):
                        return e.dma_start(
                            out=stage[:, slot * D:(slot + 1) * D],
                            in_=xs[i * 128:(i + 1) * 128, :])
                    ld[i] = P.emit("sync", f, waits=w,
                                   inc="dx%d" % (g % 4))
                evs = []
                for cT in range(8):
                    bank = ps_tr[bidx % 2]
                    for k in range(4):
                        slot = (g * 4 + k) % 8

                        def f(e, slot=slot, cT=cT, k=k, bank=bank):
                            return e.transpose(
                                out=bank[:, k * 128:(k + 1) * 128],
                                in_=stage[:, slot * D + cT * 128:
                                          slot * D + cT * 128 + 128],
                                identity=identf[:])
                        w = [ld[g * 4 + kk] for kk in range(4)]
                        w += [t_id0, bank_guard.get(bidx % 2)]
                        tk = P.emit("tensor", f, waits=w,
                                    inc="t" if k == 3 else None)

                    def fe(e, cT=cT, g=g, bank=bank):
                        return e.tensor_copy(
                            out=XR[:, cT * CW + g * 512: cT * CW + g * 512 + 512],
                            in_=bank[:])
                    ev = P.emit("vector", fe, waits=[tk], inc="v")
                    bank_guard[bidx % 2] = ev
                    evs.append(ev)
                    bidx += 1
                grp_ev[g] = evs
            ld16 = P.emit("sync", lambda e: e.dma_start(
                out=stage[0:2, 0:D], in_=xs[2048:2050, :]),
                waits=grp_ev.get(2, []), inc="dx0")
            tail_evs = []
            for cT in range(8):
                bank = ps_tr[bidx % 2]

                def f(e, cT=cT, bank=bank):
                    return e.transpose(
                        out=bank[:, 0:2],
                        in_=stage[0:2, cT * 128:cT * 128 + 128],
                        identity=identf[0:2, 0:2])
                tk = P.emit("tensor", f,
                            waits=[ld16, t_id0, bank_guard.get(bidx % 2)],
                            inc="t")

                def fe(e, cT=cT, bank=bank):
                    return e.tensor_copy(
                        out=XR[:, cT * CW + 2048: cT * CW + 2050],
                        in_=bank[:, 0:2])
                ev = P.emit("vector", fe, waits=[tk], inc="v")
                bank_guard[bidx % 2] = ev
                tail_evs.append(ev)
                bidx += 1
            xr_ready_g = grp_ev
            xr_tail = tail_evs

            wnat = [wnat0, wnat1, wnat2]
            ps_wtr = [ps_wtr0, ps_wtr1]
            w_srcs = [Wr_h, Wkf_h, Wvf_h]
            w_chs = [("dw0", "dw1"), ("dwS0", "dwS1"), ("dwv0", "dwv1")]
            wbank_guard = {}
            cast_by = []
            for wi in range(3):
                toks = []
                for half in range(2):
                    def fl(e, src=w_srcs[wi], half=half, nat=wnat[wi]):
                        return e.dma_start(
                            out=nat[:, half * 4 * D:(half + 1) * 4 * D]
                                .rearrange("p (dt c) -> p dt c", dt=4),
                            in_=bass.AP(src.tensor, half * 4 * 128 * D,
                                        [[D, 128], [128 * D, 4], [1, D]]))
                    toks.append(P.emit("gpsimd", fl, inc=w_chs[wi][half]))
                cast_by.append(toks)
            wt_slot = []
            for wi in range(3):
                nat = wnat[wi]
                ev = None
                for cT in range(8):
                    for half in range(2):
                        par = (cT * 2 + half) % 2
                        bank = ps_wtr[par]
                        for j in range(4):
                            dT = half * 4 + j

                            def ft(e, nat=nat, dT=dT, cT=cT, j=j, bank=bank):
                                return e.transpose(
                                    out=bank[:, j * 128:(j + 1) * 128],
                                    in_=nat[:, dT * D + cT * 128:
                                            dT * D + cT * 128 + 128],
                                    identity=identb[:])
                            w = [cast_by[wi][half], t_idb,
                                 wbank_guard.get(par)]
                            tk = P.emit("tensor", ft, waits=w,
                                        inc="t" if j == 3 else None)

                        def fe(e, wi=wi, cT=cT, half=half, bank=bank):
                            if wi == 2:
                                return e.activation(
                                    out=WB[wi][:, cT * D + half * 512:
                                               cT * D + half * 512 + 512],
                                    in_=bank[:], func=AF.Copy, scale=0.5)
                            return e.copy(
                                out=WB[wi][:, cT * D + half * 512:
                                           cT * D + half * 512 + 512],
                                in_=bank[:])
                        ev = P.emit("scalar", fe, waits=[tk, t_consts], inc="s")
                        wbank_guard[par] = ev
                wt_slot.append(ev)
            wt_ready_ffn1 = {"r": wt_slot[0], "k": wt_slot[1],
                             "w": wt_slot[2]}



            prevb = mu_tk
            for bi, bh in ((0, bk_h), (1, bv_h), (2, bq_h)):
                bt = row_cast(bh, prow, [prevb])
                prevb = P.emit("sync", lambda e, bi=bi: e.dma_start(
                    out=brow_sc.ap()[bi:bi + 1, :], in_=prow[:]),
                    waits=[bt], inc="drw2")
            brow_done = prevb


            ctmp = prow
            ab_tok = {}
            bankA_last = bankB_last = None
            evB = None
            dB_prev = None
            for (ABt, mu_off, wb, wtok, pn) in (
                    (AB_r, 0, WB[0], wt_slot[0], "r"),
                    (AB_k, 8, WB[1], wt_slot[1], "k")):
                for nh in range(2):
                    for cT in range(8):
                        def fm(e, mu_off=mu_off, cT=cT, wb=wb, nh=nh):
                            return e.matmul(
                                out=ps_abA[0:1, :],
                                lhsT=mu_cols[:, mu_off + cT:mu_off + cT + 1],
                                rhs=wb[:, cT * D + nh * 512:
                                       cT * D + nh * 512 + 512],
                                start=(cT == 0), stop=(cT == 7))
                        tka = P.emit("tensor", fm,
                                     waits=[muc_ev, wtok, bankA_last],
                                     inc="t" if cT == 7 else None)

                        def fc(e, cT=cT, wb=wb, nh=nh):
                            return e.matmul(
                                out=ps_abB[0:1, :],
                                lhsT=ones_col_b[:],
                                rhs=wb[:, cT * D + nh * 512:
                                       cT * D + nh * 512 + 512],
                                start=(cT == 0), stop=(cT == 7))
                        tkc = P.emit("tensor", fc,
                                     waits=[t_consts, bankB_last],
                                     inc="t" if cT == 7 else None)
                    evA = P.emit("scalar", lambda e, ABt=ABt, nh=nh: e.copy(
                        out=ABt[0:1, nh * 512:nh * 512 + 512],
                        in_=ps_abA[0:1, :]),
                        waits=[tka], inc="s")
                    bankA_last = evA
                    evC = P.emit("scalar", lambda e, nh=nh: e.copy(
                        out=ctmp[0:1, nh * 512:nh * 512 + 512],
                        in_=ps_abB[0:1, :]),
                        waits=[tkc, brow_done, evB], inc="s")
                    bankB_last = evC
                    evB = P.emit("gpsimd", lambda e, ABt=ABt, nh=nh:
                                 e.tensor_sub(
                                     out=rowB[0:1, nh * 512:nh * 512 + 512],
                                     in0=ctmp[0:1, nh * 512:nh * 512 + 512],
                                     in1=ABt[0:1, nh * 512:nh * 512 + 512]),
                                 waits=[evC, evA, dB_prev], inc="g")
                dB = P.emit("sync", lambda e, ABt=ABt: e.dma_start(
                    out=ABt[1:2, :], in_=rowB[:]), waits=[evB], inc="dab")
                dB_prev = dB
                ab_tok[pn] = dB
            ab_ready = dB

            p0_join = P.barrier()

        state["p0_join"] = p0_join
        if KDBG:
            P.emit("sync", lambda e: e.dma_start(out=dbg_p0, in_=XR[:, 0:CW]),
                   waits=p0_join, inc="ddbg")

        def make_stats(scope, rng_lo, rng_hi, inv_row, negw_dst=None,
                       negminv_bf=None, wait_cols=None, entry=(), shift_ch=None,
                       bc_dst=None, bc_psum=None):
            (cbuf, sqtmp, rs1, rs2, rs3, ps_mu) = scope
            chunks = _chunks(rng_lo, rng_hi, 256)
            toks = [None] * len(chunks)
            state = {"guard": list(entry), "bcg": None, "sqg": None}

            def emit_chunk(ci):
                lo, hi = chunks[ci]
                W = hi - lo
                wz = (wait_cols(lo, hi) if wait_cols else []) + list(entry)
                sq_wz = wz + [state["sqg"]]
                for t4 in range(4):
                    def fa(e, t4=t4, lo=lo, W=W):
                        return e.tensor_add(
                            out=cbuf[:, t4 * 512:t4 * 512 + W],
                            in0=XR[:, 2 * t4 * CW + lo:
                                   2 * t4 * CW + lo + W],
                            in1=XR[:, (2 * t4 + 1) * CW + lo:
                                   (2 * t4 + 1) * CW + lo + W])
                    tc = P.emit("vector", fa, waits=wz,
                                inc="v" if t4 == 3 else None)
                    for h2 in range(2):
                        def fs(e, t4=t4, h2=h2, lo=lo, W=W):
                            return e.activation(
                                out=sqtmp[:, (2 * t4 + h2) * 256:
                                          (2 * t4 + h2) * 256 + W],
                                in_=XR[:, (2 * t4 + h2) * CW + lo:
                                       (2 * t4 + h2) * CW + lo + W],
                                func=AF.Square)
                        ts_ = P.emit("scalar", fs, waits=sq_wz,
                                     inc="s" if h2 == 1 else None)

                    def fq(e, t4=t4, W=W):
                        return e.tensor_add(
                            out=cbuf[:, t4 * 512 + 256:t4 * 512 + 256 + W],
                            in0=sqtmp[:, 2 * t4 * 256:2 * t4 * 256 + W],
                            in1=sqtmp[:, (2 * t4 + 1) * 256:
                                      (2 * t4 + 1) * 256 + W])
                    tq = P.emit("vector", fq, waits=[ts_],
                                inc="v" if t4 == 3 else None)
                state["sqg"] = tq
                cast_toks = [tc, tq]
                for t4 in range(4):
                    def fm(e, t4=t4):
                        return e.matmul(out=ps_mu[0:1, 0:512],
                                        lhsT=invD_col_b[:],
                                        rhs=cbuf[:, t4 * 512:t4 * 512 + 512],
                                        start=(t4 == 0), stop=(t4 == 3))
                    t_mu = P.emit("tensor", fm,
                                  waits=(cast_toks + [t_consts]
                                         + state["guard"]) if t4 == 0 else [],
                                  inc="t" if t4 == 3 else None)
                e1 = P.emit("vector", lambda e, W=W: e.tensor_copy(
                    out=rs1[0:1, 0:W], in_=ps_mu[0:1, 0:W]), waits=[t_mu],
                    inc="v")
                e2 = P.emit("scalar", lambda e, W=W: e.copy(
                    out=rs2[0:1, 0:W], in_=ps_mu[0:1, 256:256 + W]),
                    waits=[t_mu, e1], inc="s")
                state["guard"] = [e1, e2]
                e3 = P.emit("vector", lambda e, W=W: e.tensor_mul(
                    out=rs3[0:1, 0:W], in0=rs1[0:1, 0:W], in1=rs1[0:1, 0:W]),
                    waits=[e1], inc="v")
                e4 = P.emit("vector", lambda e, W=W: e.tensor_sub(
                    out=rs2[0:1, 0:W], in0=rs2[0:1, 0:W], in1=rs3[0:1, 0:W]),
                    waits=[e2, e3], inc="v")
                e5 = P.emit("scalar", lambda e, W=W: e.activation(
                    out=rs3[0:1, 0:W], in_=rs2[0:1, 0:W], func=AF.Ln,
                    bias=float(EPS)), waits=[e4, t_consts], inc="s")
                e6 = P.emit("scalar", lambda e, W=W, lo=lo: e.activation(
                    out=inv_row[0:1, lo:lo + W], in_=rs3[0:1, 0:W], func=AF.Exp,
                    scale=-0.5), waits=[e5, t_consts], inc="s")
                last = e6
                if bc_dst is not None:
                    tbc = P.emit("tensor", lambda e, W=W, lo=lo: e.matmul(
                        out=bc_psum[:, 0:W], lhsT=ones_row_f[:],
                        rhs=inv_row[0:1, lo:lo + W], start=True, stop=True),
                        waits=[e6, state["bcg"]], inc="t")
                    last = P.emit("vector", lambda e, W=W, lo=lo: e.tensor_copy(
                        out=bc_dst[:, lo:lo + W], in_=bc_psum[:, 0:W]),
                        waits=[tbc], inc="v")
                    state["bcg"] = last
                if negw_dst is not None:
                    last = P.emit("vector",
                                  lambda e, W=W, lo=lo, d=negw_dst:
                                  e.scalar_tensor_tensor(
                                      out=d[0:1, lo:lo + W],
                                      in0=rs1[0:1, 0:W], scalar=-1.0,
                                      in1=inv_row[0:1, lo:lo + W],
                                      op0=OP.mult, op1=OP.mult),
                                  waits=[e6, t_consts], inc="v")
                    if lo <= 1 < hi:
                        last = P.emit("vector", lambda e, d=negw_dst:
                                      e.tensor_scalar(
                                          out=d[0:1, 1:2], in0=d[0:1, 1:2],
                                          scalar1=mask_col[0:1, :],
                                          scalar2=None, op0=OP.mult),
                                      waits=[last, c_mask], inc="v")
                    last = P.emit("sync", lambda e, d=negw_dst, lo=lo, W=W:
                                  e.dma_start(out=d[1:2, lo + 1:lo + 1 + W],
                                              in_=d[0:1, lo:lo + W]),
                                  waits=[last], inc=shift_ch + str(ci))
                if negminv_bf is not None:
                    last = P.emit("vector",
                                  lambda e, W=W, lo=lo, d=negminv_bf:
                                  e.scalar_tensor_tensor(
                                      out=d[0:1, lo:lo + W],
                                      in0=rs1[0:1, 0:W], scalar=-1.0,
                                      in1=inv_row[0:1, lo:lo + W],
                                      op0=OP.mult, op1=OP.mult),
                                  waits=[e6, t_consts], inc="v")
                toks[ci] = (lo, hi, last)
            return chunks, emit_chunk, toks

        def ffn(ln_rng, out_chunks, is_ffn2, xr_wait_fn, entry, wt_tok,
                out_record, tag):
            fst = contextlib.ExitStack()
            with fst:
                inv_row = fst.enter_context(nc.sbuf_tensor("f_inv" + tag, [1, CW], F32))
                negw2b = fst.enter_context(nc.sbuf_tensor("f_negwb" + tag, [2, CW + 4], BF16))
                rs1 = fst.enter_context(nc.sbuf_tensor("f_rs1" + tag, [1, 512], F32))
                rs2 = fst.enter_context(nc.sbuf_tensor("f_rs2" + tag, [1, 512], F32))
                rs3 = fst.enter_context(nc.sbuf_tensor("f_rs3" + tag, [1, 512], F32))
                ps_r0 = fst.enter_context(nc.psum_tensor("f_pr0" + tag, [128, 512], F32))
                ps_r1 = fst.enter_context(nc.psum_tensor("f_pr1" + tag, [128, 512], F32))
                ps_k0 = fst.enter_context(nc.psum_tensor("f_pk0" + tag, [128, 512], F32))
                ps_k1 = fst.enter_context(nc.psum_tensor("f_pk1" + tag, [128, 512], F32))
                ps_w0 = fst.enter_context(nc.psum_tensor("f_pw0" + tag, [128, 512], F32))
                ps_w1 = fst.enter_context(nc.psum_tensor("f_pw1" + tag, [128, 512], F32))
                ps_mu = fst.enter_context(nc.psum_tensor("f_pmu" + tag, [1, 512], F32))
                cbuf = fst.enter_context(nc.sbuf_tensor("f_cb" + tag, [128, 4 * 512], BF16))
                sqtmp = fst.enter_context(nc.sbuf_tensor("f_sq" + tag, [128, 8 * 256], BF16))
                inv128 = fst.enter_context(nc.sbuf_tensor("f_inv128" + tag, [128, CW], BF16))
                ps_bc = fst.enter_context(nc.psum_tensor("f_pbc" + tag, [128, 256], F32))
                ubuf = fst.enter_context(nc.sbuf_tensor("f_u" + tag, [128, NT * 516], BF16))
                dbuf = fst.enter_context(nc.sbuf_tensor("f_d" + tag, [128, NT * 512], BF16))
                xrb = fst.enter_context(nc.sbuf_tensor("f_xr" + tag, [128, NT * 512], BF16))
                xkb = fst.enter_context(nc.sbuf_tensor("f_xk" + tag, [128, NT * 512], BF16))
                rbuf = fst.enter_context(nc.sbuf_tensor("f_r" + tag, [128, NT * 512], BF16))
                kact = fst.enter_context(nc.sbuf_tensor("f_kact" + tag, [128, NT * 512], BF16))
                tmpv = fst.enter_context(nc.sbuf_tensor("f_tmpv" + tag, [128, 512], F32))
                krelu = tmpv
                natst = fst.enter_context(nc.sbuf_tensor(
                    "f_nat" + tag, [128, (D if is_ffn2 else 1)], F32))
                mz = P.emit("vector",
                            lambda e: e.memset(negw2b[0:2, 0:1], 0.0),
                            waits=list(entry), inc="v")
                scope = (cbuf, sqtmp, rs1, rs2, rs3, ps_mu)
                st_chunks, st_emit, st_toks = make_stats(
                    scope, ln_rng[0], ln_rng[1], inv_row, negw_dst=negw2b,
                    wait_cols=xr_wait_fn, entry=list(entry) + [mz],
                    shift_ch="dnw" + tag, bc_dst=inv128, bc_psum=ps_bc)
                st_n = [0]

                def st_need(hi_col):
                    while st_n[0] < len(st_chunks) and                             st_chunks[st_n[0]][0] < hi_col:
                        st_emit(st_n[0])
                        st_n[0] += 1

                ps_r = [ps_r0, ps_r1]
                ps_k = [ps_k0, ps_k1]
                wb4 = [ps_w0, ps_w1, ps_k0, ps_k1]
                last_touch = {}
                mix_done_by = {}
                d_by = {}
                state = {"prev_u": None, "prev_ureaders": []}
                natst_guard = [None]
                import os as _os
                if _os.environ.get("NO_ILV"):
                    st_need(10 ** 9)

                def prep(isc):
                    lo, hi = out_chunks[isc]
                    W = hi - lo
                    wlo = lo - 1
                    WW = hi - wlo
                    par = isc % 2
                    st_need(hi + 1)
                    u_toks = []
                    if isc == 0:
                        for cT in range(8):
                            def fu(e, cT=cT, wlo=wlo, WW=WW):
                                return e.tensor_mul(
                                    out=ubuf[:, cT * 516:cT * 516 + WW],
                                    in0=XR[:, cT * CW + wlo:
                                           cT * CW + wlo + WW],
                                    in1=inv128[:, wlo:wlo + WW])
                            wz = (xr_wait_fn(wlo, hi) if xr_wait_fn else []) \
                                + entry
                            u_toks.append(P.emit(
                                "vector", fu,
                                waits=_cover(st_toks, wlo, hi) + wz,
                                inc="v" if cT == 7 else None))
                    else:
                        off = lo - out_chunks[isc - 1][0]
                        cpy = P.emit("vector", lambda e, off=off:
                                     e.tensor_copy(
                                         out=ubuf[:, 0:7 * 516 + 1:516],
                                         in_=ubuf[:, off:off + 7 * 516 + 1:
                                                  516]),
                                     waits=[state["prev_u"]]
                                     + state["prev_ureaders"], inc="v")
                        for cT in range(8):
                            def fu(e, cT=cT, lo=lo, W=W):
                                return e.tensor_mul(
                                    out=ubuf[:, cT * 516 + 1:
                                             cT * 516 + 1 + W],
                                    in0=XR[:, cT * CW + lo: cT * CW + lo + W],
                                    in1=inv128[:, lo:lo + W])
                            wz = (xr_wait_fn(lo, hi) if xr_wait_fn else []) \
                                + entry
                            u_toks.append(P.emit(
                                "vector", fu,
                                waits=_cover(st_toks, wlo, hi) + wz + [cpy],
                                inc="v" if cT == 7 else None))
                    u_tok = u_toks[-1]
                    state["prev_u"] = u_tok
                    if wlo <= 1 < hi:
                        idx = 1 - wlo
                        for cT in range(8):
                            def fm(e, cT=cT, idx=idx):
                                return e.tensor_scalar(
                                    out=ubuf[:, cT * 516 + idx:
                                             cT * 516 + idx + 1],
                                    in0=ubuf[:, cT * 516 + idx:
                                             cT * 516 + idx + 1],
                                    scalar1=mask_col[:], scalar2=None,
                                    op0=OP.mult)
                            u_tok = P.emit("vector", fm,
                                           waits=[u_tok, c_mask],
                                           inc="v" if cT == 7 else None)
                    for cT in range(8):
                        def fd(e, cT=cT, W=W):
                            return e.tensor_sub(
                                out=dbuf[:, cT * 512:cT * 512 + W],
                                in0=ubuf[:, cT * 516 + 1:cT * 516 + 1 + W],
                                in1=ubuf[:, cT * 516:cT * 516 + W])
                        d_tok = P.emit("vector", fd, waits=[u_tok],
                                       inc="v" if cT == 7 else None)
                    d_by[isc] = d_tok
                    state["prev_ureaders"] = [d_tok]
                    if isc == 0:
                        emit_mix(0, "r", None)
                        emit_mix(0, "k", None)

                def emit_mix(isc, key, extra):
                    lo, hi = out_chunks[isc]
                    W = hi - lo
                    dst, mu_off = (xrb, 0) if key == "r" else (xkb, 8)
                    for cT in range(8):
                        def fx(e, dst=dst, cT=cT, mu_off=mu_off, W=W):
                            return e.scalar_tensor_tensor(
                                out=dst[:, cT * 512:cT * 512 + W],
                                in0=dbuf[:, cT * 512:cT * 512 + W],
                                scalar=mu_cols[:, mu_off + cT:
                                               mu_off + cT + 1],
                                in1=ubuf[:, cT * 516:cT * 516 + W],
                                op0=OP.mult, op1=OP.add)
                        mt = P.emit("vector", fx,
                                    waits=[d_by[isc], muc_ev, extra],
                                    inc="v" if cT == 7 else None)
                    mix_done_by.setdefault(isc, {})[key] = mt
                    state["prev_ureaders"].append(mt)

                def compute(isc):
                    lo, hi = out_chunks[isc]
                    W = hi - lo
                    par = isc % 2
                    mix_done = mix_done_by[isc]
                    prev_out = last_touch.get(("outlast", isc - 1))
                    evac_sig = {}
                    for (pname, psl, wslot, ABt) in (("r", ps_r, WB[0], AB_r),
                                                     ("k", ps_k, WB[1], AB_k)):
                        for dC in range(8):
                            bank = psl[dC % 2]
                            for cT in range(8):
                                def fm(e, wslot=wslot, cT=cT, dC=dC, bank=bank,
                                       pname=pname, W=W):
                                    rhs = xrb if pname == "r" else xkb
                                    return e.matmul(
                                        out=bank[:, 0:W],
                                        lhsT=wslot[:, cT * D + dC * 128:
                                                   cT * D + dC * 128 + 128],
                                        rhs=rhs[:, cT * 512:cT * 512 + W],
                                        start=(cT == 0), stop=False)
                                w = ([mix_done[pname],
                                      last_touch.get((pname, dC % 2)),
                                      wt_tok[pname], ab_tok[pname]]
                                     + _cover(st_toks, lo - 1, hi)
                                     if cT == 0 else [])
                                P.emit("tensor", fm, waits=w)

                            def fk2(e, ABt=ABt, dC=dC, bank=bank, lo=lo, W=W):
                                return e.matmul(
                                    out=bank[:, 0:W],
                                    lhsT=ABt[0:2, dC * 128:dC * 128 + 128],
                                    rhs=negw2b[0:2, lo:lo + W],
                                    start=False, stop=True)
                            tk = P.emit("tensor", fk2, inc="t")
                            if pname == "r":
                                def fe(e, dC=dC, bank=bank, W=W):
                                    return e.activation(
                                        out=rbuf[:, dC * 512:dC * 512 + W],
                                        in_=bank[:, 0:W], func=AF.Sigmoid)
                                ev = P.emit("scalar", fe,
                                            waits=[tk, last_touch.get(
                                                ("rbuf", dC))], inc="s")
                            else:
                                def fr(e, bank=bank, W=W):
                                    return e.activation(
                                        out=krelu[:, 0:W], in_=bank[:, 0:W],
                                        func=AF.Relu)
                                ev0 = P.emit("scalar", fr,
                                             waits=[tk, prev_out,
                                                    evac_sig.get(("k",
                                                                  dC - 1))],
                                             inc="s")

                                def fs(e, dC=dC, W=W):
                                    return e.activation(
                                        out=kact[:, dC * 512:dC * 512 + W],
                                        in_=krelu[:, 0:W], func=AF.Square)
                                ev = P.emit("scalar", fs, waits=[ev0], inc="s")
                            last_touch[(pname, dC % 2)] = ev
                            evac_sig[(pname, dC)] = ev
                        last_touch[("mmdone", pname, isc)] = \
                            evac_sig[(pname, 7)]
                        if isc + 1 < len(out_chunks):
                            emit_mix(isc + 1, pname, evac_sig[(pname, 7)])
                    if isc + 2 < len(out_chunks):
                        st_need(out_chunks[isc + 2][1] + 1)

                    out_sig = []
                    for eC in range(8):
                        bank = wb4[eC % 4]
                        for dT in range(8):
                            def fm(e, dT=dT, eC=eC, bank=bank, W=W):
                                return e.matmul(
                                    out=bank[:, 0:W],
                                    lhsT=WB[2][:, dT * D + eC * 128:
                                               dT * D + eC * 128 + 128],
                                    rhs=kact[:, dT * 512:dT * 512 + W],
                                    start=(dT == 0), stop=(dT == 7))
                            w = [] if dT else [evac_sig[("k", 7)],
                                               last_touch.get(("w", eC % 4)),
                                               last_touch.get(("k", eC % 2))
                                               if eC % 4 >= 2 else None,
                                               wt_tok["w"]]
                            tk = P.emit("tensor", fm, waits=w,
                                        inc="t" if dT == 7 else None)

                        def fmul(e, bank=bank, eC=eC, W=W):
                            return e.tensor_mul(
                                out=tmpv[:, 0:W], in0=bank[:, 0:W],
                                in1=rbuf[:, eC * 512:eC * 512 + W])
                        ev0 = P.emit("vector", fmul,
                                     waits=[tk, evac_sig[("r", eC)],
                                            out_sig[-1] if out_sig else None],
                                     inc="v")
                        last_touch[("rbuf", eC)] = ev0
                        last_touch[("w", eC % 4)] = ev0
                        if eC % 4 >= 2:
                            last_touch[("k", eC % 2)] = ev0

                        def fadd(e, eC=eC, lo=lo, W=W):
                            return e.tensor_add(
                                out=XR[:, eC * CW + lo: eC * CW + lo + W],
                                in0=XR[:, eC * CW + lo: eC * CW + lo + W],
                                in1=tmpv[:, 0:W])
                        out_sig.append(P.emit("vector", fadd, waits=[ev0],
                                              inc="v"))
                    out_record[isc] = list(out_sig)
                    last_touch[("outlast", isc)] = out_sig[-1]

                    if is_ffn2:
                        for tT in range(4):
                            evns = []
                            for g2 in range(2):
                                rpar = (tT * 2 + g2) % 2
                                bank = ps_r[rpar]
                                for k in range(4):
                                    cT = g2 * 4 + k

                                    def ft(e, cT=cT, tT=tT, k=k, bank=bank,
                                           lo=lo):
                                        return e.transpose(
                                            out=bank[:, k * 128:(k + 1) * 128],
                                            in_=XR[:, cT * CW + lo + tT * 128:
                                                   cT * CW + lo + tT * 128
                                                   + 128],
                                            identity=identf[:])
                                    w = [out_sig[cT],
                                         last_touch.get(("r", rpar))]
                                    tk = P.emit("tensor", ft, waits=w,
                                                inc="t" if k == 3 else None)

                                def fe(e, g2=g2, bank=bank):
                                    return e.tensor_copy(
                                        out=natst[:, g2 * 512:g2 * 512 + 512],
                                        in_=bank[:])
                                evn = P.emit("vector", fe,
                                             waits=[tk, natst_guard[0]],
                                             inc="v")
                                last_touch[("r", rpar)] = evn
                                evns.append(evn)

                            def fdo(e, tT=tT, lo=lo):
                                r0 = (lo - 2) + tT * 128
                                return e.dma_start(
                                    out=y_out[r0:r0 + 128, :],
                                    in_=natst[:])
                            natst_guard[0] = P.emit("sync", fdo, waits=evns,
                                                    inc="dout")

                prep(0)
                for isc in range(len(out_chunks)):
                    if isc + 1 < len(out_chunks):
                        prep(isc + 1)
                    compute(isc)
        def xr_wait_p1(lo, hi):
            toks = []
            for g in range(4):
                if lo < (g + 1) * 512 and hi > g * 512:
                    toks.extend(xr_ready_g[g])
            if hi > 2048:
                toks.extend(xr_tail)
            return toks

        ffn1_chunks = [(1, 513), (513, 1025), (1025, 1537), (1537, 2049),
                       (2049, 2050)]
        ffn1_out = {}
        ffn((0, TL), ffn1_chunks, False, xr_wait_p1, p0_join, wt_ready_ffn1,
            ffn1_out, "1")

        pk_guard = None
        pk_i = 0
        for wi, src_h in ((0, Wk_h), (1, Wv_h), (2, Wq_h)):
            for q4 in range(4):
                thr = ffn1_out[min(1 + pk_i // 6, 2)][-1]
                pk_i += 1

                def fl(e, src_h=src_h, q4=q4):
                    return e.dma_start(
                        out=park[:].rearrange("p (dt c) -> p dt c", dt=2),
                        in_=bass.AP(src_h.tensor, q4 * 2 * 128 * D,
                                    [[D, 128], [128 * D, 2], [1, D]]))
                ldp = P.emit("gpsimd", fl, waits=[pk_guard, thr], inc="dpk0")

                def fs(e, wi=wi, q4=q4):
                    return e.dma_start(
                        out=wattn.ap()[wi * 128:(wi + 1) * 128,
                                       q4 * 2 * D:(q4 + 1) * 2 * D],
                        in_=park[:])
                pk_guard = P.emit("sync", fs, waits=[ldp], inc="dpk1")
        attn_sc_done = [("dpk0", P.cnt["dpk0"]), ("dpk1", P.cnt["dpk1"])]

        b1 = P.barrier()
        if KDBG:
            P.emit("sync", lambda e: e.dma_start(out=dbg_f1, in_=XR[:, 0:CW]),
                   waits=b1, inc="ddbg")

        ast = contextlib.ExitStack()
        with ast:
            inv_row2 = ast.enter_context(nc.sbuf_tensor("a_inv", [1, CW], F32))
            negminvb = ast.enter_context(nc.sbuf_tensor("a_negmb", [1, CW], BF16))
            inv128bc = ast.enter_context(nc.sbuf_tensor("a_inv128", [128, CW], F32))
            colsk = ast.enter_context(nc.sbuf_tensor("a_colsk", [1, D], BF16))
            colsv = ast.enter_context(nc.sbuf_tensor("a_colsv", [1, D], BF16))
            bk_row = ast.enter_context(nc.sbuf_tensor("bkrow", [1, D], BF16))
            bv_row = ast.enter_context(nc.sbuf_tensor("bvrow", [1, D], BF16))
            ps_a0 = ast.enter_context(nc.psum_tensor("a_pa0", [128, 512], F32))
            ps_a1 = ast.enter_context(nc.psum_tensor("a_pa1", [128, 512], F32))
            ps_b0 = ast.enter_context(nc.psum_tensor("a_pb0", [128, 512], F32))
            ps_b1 = ast.enter_context(nc.psum_tensor("a_pb1", [128, 512], F32))
            ps_c0 = ast.enter_context(nc.psum_tensor("a_pc0", [128, 512], F32))
            ps_c1 = ast.enter_context(nc.psum_tensor("a_pc1", [128, 512], F32))
            ps_d0 = ast.enter_context(nc.psum_tensor("a_pd0", [128, 512], F32))
            ps_d1 = ast.enter_context(nc.psum_tensor("a_pd1", [128, 512], F32))
            c_bk = P.emit("sync", lambda e: e.dma_start(
                out=bk_row[:], in_=brow_sc.ap()[0:1, :]), waits=b1 + [brow_done],
                inc="dbias")
            c_bv = P.emit("sync", lambda e: e.dma_start(
                out=bv_row[:], in_=brow_sc.ap()[1:2, :]), waits=[c_bk],
                inc="dbias")
            bias_all = ("dbias", P.cnt["dbias"])
            c_bk = c_bv = bias_all

            def xr_wait_p2(lo, hi):
                toks = list(b1)
                for i, (clo, chi) in enumerate(ffn1_chunks):
                    if lo < chi and hi > clo:
                        toks.extend(ffn1_out[i])
                return toks

            s2st = contextlib.ExitStack()
            with s2st:
                cbuf2 = s2st.enter_context(nc.sbuf_tensor("a_cb", [128, 4 * 512], BF16))
                sqtmp2 = s2st.enter_context(nc.sbuf_tensor("a_sq", [128, 8 * 256], BF16))
                rs1b = s2st.enter_context(nc.sbuf_tensor("a_rs1", [1, 512], F32))
                rs2b = s2st.enter_context(nc.sbuf_tensor("a_rs2", [1, 512], F32))
                rs3b = s2st.enter_context(nc.sbuf_tensor("a_rs3", [1, 512], F32))
                scope2 = (cbuf2, sqtmp2, rs1b, rs2b, rs3b, ps_a0)
                _c2, _e2, st2 = make_stats(scope2, 1, TL, inv_row2,
                                           negminv_bf=negminvb,
                                           wait_cols=xr_wait_p2, entry=b1,
                                           bc_dst=inv128bc, bc_psum=ps_a1)
                for _ci in range(len(_c2)):
                    _e2(_ci)
            st2t = [t for (_a, _b, t) in st2]

            XBF = ast.enter_context(nc.sbuf_tensor("a_XBF", [128, NT * 516], BF16))
            kvst = contextlib.ExitStack()
            K8 = kvst.enter_context(nc.sbuf_tensor("a_K8", [128, 4 * D], BF16))
            V8 = kvst.enter_context(nc.sbuf_tensor("a_V8", [128, 4 * D], BF16))
            C_sb = kvst.enter_context(nc.sbuf_tensor("a_Csb", [128, NT * D], F32))

            bc_chunks = _chunks(1, TL)
            bc_done_inv = st2t[-1]

            spill_chs = ["dspill", "dsp2", "dsp3"]
            load_chs = ["dwA", "dwA2", "dwA3"]
            spill_toks = []
            for i in range(3):
                def fsp(e, i=i):
                    return e.dma_start(
                        out=wspill.ap()[i * 128:(i + 1) * 128, :], in_=WB[i][:])
                spill_toks.append(P.emit("sync", fsp, waits=b1,
                                         inc=spill_chs[i]))
            spill_all = list(spill_toks)
            attn_w = []
            for i in range(3):
                def fl(e, i=i):
                    return e.dma_start(
                        out=WB[i][:],
                        in_=wattn.ap()[i * 128:(i + 1) * 128, :])
                attn_w.append(P.emit("sync", fl,
                                     waits=[spill_toks[i], attn_sc_done],
                                     inc=load_chs[i]))

            cs_prev = None
            for (dst, wi) in ((colsk, 0), (colsv, 1)):
                for nh in range(2):
                    for dT in range(8):
                        def fm(e, wi=wi, dT=dT, nh=nh):
                            return e.matmul(
                                out=ps_b0[0:1, 0:512], lhsT=ones_col_b[:],
                                rhs=WB[wi][:, dT * D + nh * 512:
                                           dT * D + nh * 512 + 512],
                                start=(dT == 0), stop=(dT == 7))
                        tk = P.emit("tensor", fm,
                                    waits=[attn_w[wi], t_consts, cs_prev],
                                    inc="t" if dT == 7 else None)
                    cs_prev = P.emit("scalar",
                                     lambda e, dst=dst, nh=nh: e.copy(
                                         out=dst[0:1, nh * 512:nh * 512 + 512],
                                         in_=ps_b0[0:1, 0:512]),
                                     waits=[tk], inc="s")
            cs_done = cs_prev

            kv_banks = [ps_a0, ps_a1, ps_b0, ps_b1]
            kv_bank_guard = {0: bc_done_inv, 1: st2t[-1], 2: cs_done, 3: cs_done}
            C_banks = {(0, 0): ps_c0, (0, 1): ps_c1, (1, 0): ps_d0,
                       (1, 1): ps_d1}
            c_bank_guard = {}
            c_ev_mc = {}
            stg_guard = {}
            kv_last = {}
            cdma_prev = None
            bank_i = 0
            for q in range(4):
                wlo = 1 + 512 * q
                xb_toks = []
                for dT in range(8):
                    def fx(e, dT=dT, wlo=wlo):
                        return e.tensor_mul(
                            out=XBF[:, dT * 516:dT * 516 + 513],
                            in0=XR[:, dT * CW + wlo: dT * CW + wlo + 513],
                            in1=inv128bc[:, wlo:wlo + 513])
                    wz = xr_wait_p2(wlo, wlo + 513) + \
                        _cover(st2, wlo, wlo + 513) + [kv_last.get(q - 1)]
                    xb_toks.append(P.emit("vector", fx, waits=wz,
                                          inc="v" if dT == 7 else None))
                xb_tok = xb_toks[-1]
                ev_kv = {}
                for tt in range(4):
                    gcol = 2 + 512 * q + 128 * tt
                    lcol = gcol - wlo
                    for (kv_i, wslot, colrow, cbias) in ((0, 0, colsk, c_bk),
                                                         (1, 1, colsv, c_bv)):
                        for nh in range(2):
                            bank = kv_banks[bank_i % 4]
                            this_bank = bank_i % 4
                            bank_i += 1
                            for dT in range(8):
                                def fm(e, dT=dT, nh=nh, wslot=wslot,
                                       lcol=lcol, bank=bank):
                                    return e.matmul(
                                        out=bank[:],
                                        lhsT=XBF[:, dT * 516 + lcol:
                                                 dT * 516 + lcol + 128],
                                        rhs=WB[wslot][:, dT * D + nh * 512:
                                                      dT * D + nh * 512 + 512],
                                        start=(dT == 0), stop=False)
                                w = ([xb_tok, attn_w[wslot],
                                      kv_bank_guard.get(this_bank),
                                      cbias, cs_done]
                                     + _cover(st2, gcol, gcol + 128)
                                     ) if dT == 0 else []
                                P.emit("tensor", fm, waits=w)
                            brow = bk_row if kv_i == 0 else bv_row

                            def fb1(e, nh=nh, brow=brow, bank=bank):
                                return e.matmul(
                                    out=bank[:], lhsT=ones_row_b[:],
                                    rhs=brow[0:1, nh * 512:nh * 512 + 512],
                                    start=False, stop=False)
                            P.emit("tensor", fb1)

                            def fb2(e, nh=nh, colrow=colrow, gcol=gcol,
                                    bank=bank):
                                return e.matmul(
                                    out=bank[:],
                                    lhsT=negminvb[0:1, gcol:gcol + 128],
                                    rhs=colrow[0:1, nh * 512:nh * 512 + 512],
                                    start=False, stop=True)
                            tk = P.emit("tensor", fb2, inc="t")
                            if kv_i == 0:
                                def fe(e, tt=tt, nh=nh, bank=bank):
                                    return e.copy(
                                        out=K8[:, tt * D + nh * 512:
                                               tt * D + nh * 512 + 512],
                                        in_=bank[:])
                                ev = P.emit("scalar", fe, waits=[tk], inc="s")
                            else:
                                def fe(e, tt=tt, nh=nh, bank=bank):
                                    return e.tensor_copy(
                                        out=V8[:, tt * D + nh * 512:
                                               tt * D + nh * 512 + 512],
                                        in_=bank[:])
                                ev = P.emit("vector", fe, waits=[tk], inc="v")
                            kv_bank_guard[this_bank] = ev
                            ev_kv[(kv_i, tt, nh)] = ev
                kv_done = [ev_kv[k] for k in ev_kv]
                kv_last[q] = ev_kv[(1, 3, 1)]
                for mc in range(8):
                    for nh in range(2):
                        bank = C_banks[(mc % 2, nh)]
                        for tt in range(4):
                            def fm(e, mc=mc, nh=nh, tt=tt, bank=bank):
                                return e.matmul(
                                    out=bank[:],
                                    lhsT=K8[:, tt * D + mc * 128:
                                            tt * D + mc * 128 + 128],
                                    rhs=V8[:, tt * D + nh * 512:
                                           tt * D + nh * 512 + 512],
                                    start=(tt == 0), stop=(tt == 3))
                            w = kv_done + [c_bank_guard.get((mc % 2, nh))]
                            tk = P.emit("tensor", fm, waits=w,
                                        inc="t" if tt == 3 else None)

                        def fe(e, bank=bank, mc=mc, nh=nh, q=q):
                            dst = C_sb[:, mc * D + nh * 512:
                                       mc * D + nh * 512 + 512]
                            if q == 0:
                                return e.tensor_copy(out=dst, in_=bank[:])
                            return e.tensor_add(out=dst, in0=dst, in1=bank[:])
                        ev = P.emit("vector", fe, waits=[tk], inc="v")
                        c_bank_guard[(mc % 2, nh)] = ev
                        c_ev_mc[(mc, nh)] = ev
                        cdma_prev = ev

            cw_tok = {}
            for mc in range(8):
                for rep in range(2):
                    def fw(e, mc=mc, rep=rep):
                        return e.dma_start(
                            out=bass.AP(Cdd, mc * 128 * 2048 + rep * 1024,
                                        [[2048, 128], [1, 1024]]),
                            in_=C_sb[:, mc * D:(mc + 1) * D])
                    cw_tok[mc] = P.emit("gpsimd", fw,
                                        waits=[c_ev_mc[(mc, 0)],
                                               c_ev_mc[(mc, 1)]],
                                        inc="dC%d" % mc)
            dup = cw_tok[7]
            kvst.close()
            rot_all = ast.enter_context(nc.sbuf_tensor("a_rot", [128, 4 * D], BF16))
            minv128bc = ast.enter_context(nc.sbuf_tensor("a_minv128", [128, CW], F32))
            srow = ast.enter_context(nc.sbuf_tensor("a_srow", [1, D], F32))
            srowb = ast.enter_context(nc.sbuf_tensor("a_srowb", [1, D], BF16))
            bq_colt = ast.enter_context(nc.sbuf_tensor("a_bqc", [128, 8], BF16))
            colq_c = ast.enter_context(nc.sbuf_tensor("a_colq", [128, 8], F32))
            bqS_c = ast.enter_context(nc.sbuf_tensor("a_bqS", [128, 8], F32))
            tmpa = [ast.enter_context(nc.sbuf_tensor(
                "a_tmpa%d" % i, [128, 512], F32)) for i in range(2)]
            bq_row = ast.enter_context(nc.sbuf_tensor("bqrow", [1, D], BF16))
            c_bq = P.emit("sync", lambda e: e.dma_start(
                out=bq_row[:], in_=brow_sc.ap()[2:3, :]),
                waits=[bias_all, kv_last[3], cdma_prev]
                + list(cw_tok.values()), inc="dbias")

            wq_banks = [ps_c0[:].bitcast(BF16), ps_c1[:].bitcast(BF16)]
            prevq = {0: c_bank_guard.get((0, 0)), 1: c_bank_guard.get((1, 0))}
            wq_entry = [cdma_prev, kv_last[3], attn_w[2], t_idb]
            wqT_done = None
            for eT in range(8):
                for half in range(2):
                    par = (eT * 2 + half) % 2
                    bank = wq_banks[par]
                    for j in range(4):
                        dT = half * 4 + j

                        def ft(e, eT=eT, dT=dT, j=j, bank=bank):
                            return e.transpose(
                                out=bank[:, j * 128:(j + 1) * 128],
                                in_=WB[2][:, dT * D + eT * 128:
                                          dT * D + eT * 128 + 128],
                                identity=identb[:])
                        w = wq_entry + [prevq.get(par)]
                        tk = P.emit("tensor", ft, waits=w,
                                    inc="t" if j == 3 else None)

                    def fe(e, eT=eT, half=half, bank=bank):
                        return e.copy(
                            out=WB[0][:, eT * D + half * 512:
                                      eT * D + half * 512 + 512],
                            in_=bank[:, 0:512])
                    prevq[par] = P.emit("scalar", fe, waits=[tk], inc="s")
                    wqT_done = prevq[par]

            ev_s0 = ev_s1 = None
            s_mm_prev = None
            for rnd in range(2):
                rd = None
                for m4 in range(4):
                    mT = rnd * 4 + m4

                    def fr(e, mT=mT, m4=m4):
                        src = bass.AP(Cdd, mT * 128 * 2047 + 1024,
                                      [[2047, 128], [1, 1024]])
                        return e.dma_start(
                            out=rot_all[:, m4 * D:(m4 + 1) * D], in_=src)
                    rd = P.emit("sync", fr,
                                waits=[cw_tok[mT], s_mm_prev],
                                inc="drot%d" % rnd)
                s_mm = None
                for nh in range(2):
                    for m4 in range(4):
                        def fm(e, m4=m4, nh=nh):
                            bank = ps_a0 if nh == 0 else ps_a1
                            return e.matmul(
                                out=bank[0:1, 0:512], lhsT=ones_col_b[:],
                                rhs=rot_all[:, m4 * D + nh * 512:
                                            m4 * D + nh * 512 + 512],
                                start=(m4 == 0), stop=(m4 == 3))
                        w = ([rd, t_consts, kv_bank_guard.get(0),
                              kv_bank_guard.get(1), ev_s0, ev_s1]
                             if m4 == 0 else [])
                        s_mm = P.emit("tensor", fm, waits=w,
                                      inc="t" if m4 == 3 else None)
                s_mm_prev = s_mm
                if rnd == 0:
                    ev_s0 = P.emit("vector", lambda e: e.tensor_copy(
                        out=srow[0:1, 0:512], in_=ps_a0[0:1, 0:512]),
                        waits=[s_mm], inc="v")
                    ev_s1 = P.emit("vector", lambda e: e.tensor_copy(
                        out=srow[0:1, 512:1024], in_=ps_a1[0:1, 0:512]),
                        waits=[s_mm], inc="v")
                else:
                    ev_s0 = P.emit("vector", lambda e: e.tensor_add(
                        out=srow[0:1, 0:512], in0=srow[0:1, 0:512],
                        in1=ps_a0[0:1, 0:512]), waits=[s_mm, ev_s0], inc="v")
                    ev_s1 = P.emit("vector", lambda e: e.tensor_add(
                        out=srow[0:1, 512:1024], in0=srow[0:1, 512:1024],
                        in1=ps_a1[0:1, 0:512]), waits=[s_mm, ev_s1], inc="v")
            cc_stage = ast.enter_context(nc.sbuf_tensor("a_ccst", [128, 128], F32))
            z_cc = P.emit("vector", lambda e: e.memset(cc_stage[:], 0.0),
                          waits=[kv_last[3], cdma_prev]
                          + list(cw_tok.values()), inc="v")
            d_s0 = P.emit("gpsimd", lambda e: e.dma_start(
                out=cc_stage[0:8, :], in_=srow[:]),
                waits=[ev_s0, ev_s1, z_cc], inc="dgs")
            d_s = P.emit("gpsimd", lambda e: e.dma_start(
                out=cc_in.ap(), in_=cc_stage[:]), waits=[d_s0],
                inc="dgs")
            import os as _os
            if _os.environ.get("NO_CC"):
                t_cc = P.emit("gpsimd", lambda e: e.dma_start(
                    out=cc_out.ap(), in_=cc_in.ap()), waits=[d_s], inc="cc")
            else:
                t_cc = P.emit("gpsimd", lambda e: e.collective_compute(
                    "AllReduce", OP.add,
                    replica_groups=[[0, 1], [2, 3], [4, 5], [6, 7]],
                    ins=[cc_in.ap().opt()], outs=[cc_out.ap().opt()]),
                    waits=[d_s], inc="cc")
            d_s2 = P.emit("gpsimd", lambda e: e.dma_start(
                out=srow[:], in_=cc_out.ap()[0:8, :]),
                waits=[t_cc, ev_s0, ev_s1], inc="dgs")
            ev_sb = P.emit("vector", lambda e: e.tensor_copy(
                out=srowb[:], in_=srow[:]), waits=[d_s2], inc="v")
            prev_mv = [ev_s0, ev_s1]
            for ci, (lo, hi) in enumerate(bc_chunks):
                W = hi - lo

                def fb(e, lo=lo, W=W):
                    return e.matmul(out=ps_a0[:, 0:W], lhsT=negones_row_b[:],
                                    rhs=negminvb[0:1, lo:lo + W],
                                    start=True, stop=True)
                tb = P.emit("tensor", fb,
                            waits=_cover(st2, lo, hi) + prev_mv, inc="t")
                prev_mv = [P.emit("vector",
                                  lambda e, lo=lo, W=W: e.tensor_copy(
                                      out=minv128bc[:, lo:lo + W],
                                      in_=ps_a0[:, 0:W]), waits=[tb], inc="v")]
            bc_done = prev_mv[0]
            d_sd0 = P.emit("sync", lambda e: e.dma_start(
                out=sdd.ap()[0:1, 0:1024], in_=srowb[:]), waits=[ev_sb],
                inc="dsdd")
            d_sd1 = P.emit("sync", lambda e: e.dma_start(
                out=sdd.ap()[0:1, 1024:2048], in_=srowb[:]), waits=[ev_sb],
                inc="dsdd")

            s_ready = None
            for eT in range(8):
                def fS(e, eT=eT):
                    src = bass.AP(sdd, eT * 128, [[1, 128], [1, 1024]])
                    return e.dma_start(out=WB[1][:, eT * D:(eT + 1) * D],
                                       in_=src)
                s_ready = P.emit("sync", fS,
                                 waits=[d_sd0, d_sd1, kv_last[3]], inc="dS")

            qp_banks = [ps_b0, ps_b1]
            prevp = {0: kv_bank_guard[2], 1: kv_bank_guard[3]}
            wqp_done = None
            for dC in range(8):
                for nh in range(2):
                    par = (dC * 2 + nh) % 2
                    bank = qp_banks[par]
                    for eT in range(8):
                        def fm(e, dC=dC, nh=nh, eT=eT, bank=bank):
                            return e.matmul(
                                out=bank[:],
                                lhsT=WB[0][:, eT * D + dC * 128:
                                           eT * D + dC * 128 + 128],
                                rhs=WB[1][:, eT * D + nh * 512:
                                          eT * D + nh * 512 + 512],
                                start=(eT == 0), stop=(eT == 7))
                        w = [wqT_done, s_ready, prevp.get(par)]
                        tk = P.emit("tensor", fm, waits=w,
                                    inc="t" if eT == 7 else None)

                    def fe(e, dC=dC, nh=nh, bank=bank):
                        return e.copy(
                            out=WB[2][:, dC * D + nh * 512:
                                      dC * D + nh * 512 + 512],
                            in_=bank[:])
                    prevp[par] = P.emit("scalar", fe, waits=[tk], inc="s")
                    wqp_done = prevp[par]

            bq_bank = ps_c0[:].bitcast(BF16)
            for j in range(8):
                def ft(e, j=j):
                    return e.transpose(out=bq_bank[:, 2 * j:2 * j + 1],
                                       in_=bq_row[0:1, j * 128:j * 128 + 128],
                                       identity=identb[0:1, 0:1])
                tbq = P.emit("tensor", ft, waits=[c_bq, t_idb, wqT_done],
                             inc="t" if j == 7 else None)
            ev_bqc = P.emit("vector", lambda e: e.tensor_copy(
                out=bq_colt[:], in_=bq_bank[:, 0:16:2]), waits=[tbq], inc="v")
            for nC in range(8):
                for dT in range(8):
                    def fm(e, nC=nC, dT=dT):
                        return e.matmul(
                            out=ps_c1[:, nC:nC + 1],
                            lhsT=WB[2][:, dT * D + nC * 128:
                                       dT * D + nC * 128 + 128],
                            rhs=ones_col_b[:],
                            start=(dT == 0), stop=(dT == 7))
                    tcq = P.emit("tensor", fm,
                                 waits=[wqp_done, t_consts,
                                        c_bank_guard.get((1, 0))],
                                 inc="t" if (nC == 7 and dT == 7) else None)
                for eT in range(8):
                    def fm2(e, nC=nC, eT=eT):
                        return e.matmul(
                            out=ps_d0[:, nC:nC + 1],
                            lhsT=WB[1][:, eT * D + nC * 128:
                                       eT * D + nC * 128 + 128],
                            rhs=bq_colt[:, eT:eT + 1],
                            start=(eT == 0), stop=(eT == 7))
                    tbs = P.emit("tensor", fm2,
                                 waits=[ev_bqc, s_ready,
                                        c_bank_guard.get((1, 0))],
                                 inc="t" if (nC == 7 and eT == 7) else None)
            ev_cq = P.emit("scalar", lambda e: e.mul(
                out=colq_c[:], in_=ps_c1[:, 0:8], mul=-1.0),
                waits=[tcq, t_consts], inc="s")
            ev_bqs = P.emit("vector", lambda e: e.tensor_copy(
                out=bqS_c[:], in_=ps_d0[:, 0:8]), waits=[tbs], inc="v")

            if KDBG:
                for wi3 in range(3):
                    P.emit("sync", lambda e, wi3=wi3: e.dma_start(
                        out=dbg_wq[wi3 * 128:(wi3 + 1) * 128, :],
                        in_=WB[wi3][:]),
                        waits=[wqp_done, s_ready, wqT_done, ev_bqs, ev_cq],
                        inc="ddbg")
                P.emit("sync", lambda e: e.dma_start(
                    out=dbg_bc[0:128, :], in_=inv128bc[:]),
                    waits=[bc_done_inv], inc="ddbg")
                P.emit("sync", lambda e: e.dma_start(
                    out=dbg_bc[128:256, :], in_=minv128bc[:]),
                    waits=[bc_done], inc="ddbg")
                P.emit("sync", lambda e: e.dma_start(
                    out=dbg_cq[:, 0:8], in_=colq_c[:]),
                    waits=[ev_cq], inc="ddbg")
                P.emit("sync", lambda e: e.dma_start(
                    out=dbg_cq[:, 8:16], in_=bqS_c[:]),
                    waits=[ev_bqs], inc="ddbg")
                P.emit("sync", lambda e: e.dma_start(
                    out=dbg_cq[:, 16:24].bitcast(mybir.dt.bfloat16)[:, 0:8],
                    in_=bq_colt[:]),
                    waits=[ev_bqc], inc="ddbg")

            ao_chunks = [(1, 513), (513, 1025), (1025, 1537), (1537, 2049),
                         (2049, 2050)]
            ao_done = {}
            ao_bank_guard = {0: c_bank_guard.get((0, 1)),
                             1: c_bank_guard.get((1, 1))}
            tmpa_guard = {}
            ao_prev = None
            for j, (lo, hi) in enumerate(ao_chunks):
                W = hi - lo
                xb_toks = []
                for dT in range(8):
                    def fx(e, dT=dT, lo=lo, W=W):
                        return e.tensor_mul(
                            out=XBF[:, dT * 516:dT * 516 + W],
                            in0=XR[:, dT * CW + lo: dT * CW + lo + W],
                            in1=inv128bc[:, lo:lo + W])
                    wz = [bc_done, kv_last[3], ao_prev]
                    xb_toks.append(P.emit("vector", fx, waits=wz,
                                          inc="v" if dT == 7 else None))
                xb_tok = xb_toks[-1]
                outs = []
                for nC in range(8):
                    par = nC % 2
                    bank = [ps_a0, ps_a1][par]
                    for dT in range(8):
                        def fm(e, nC=nC, dT=dT, W=W, bank=bank):
                            return e.matmul(
                                out=bank[:, 0:W],
                                lhsT=WB[2][:, dT * D + nC * 128:
                                           dT * D + nC * 128 + 128],
                                rhs=XBF[:, dT * 516:dT * 516 + W],
                                start=(dT == 0), stop=(dT == 7))
                        w = [xb_tok, wqp_done, ao_bank_guard.get(par),
                             ev_s0, ev_s1]
                        tk = P.emit("tensor", fm, waits=w,
                                    inc="t" if dT == 7 else None)

                    def fe1(e, nC=nC, lo=lo, W=W, bank=bank):
                        return e.scalar_tensor_tensor(
                            out=tmpa[nC % 2][:, 0:W],
                            in0=minv128bc[:, lo:lo + W],
                            scalar=colq_c[:, nC:nC + 1], in1=bank[:, 0:W],
                            op0=OP.mult, op1=OP.add)
                    e1 = P.emit("vector", fe1,
                                waits=[tk, ev_cq, bc_done,
                                       tmpa_guard.get(nC % 2)],
                                inc="v")
                    ao_bank_guard[par] = e1

                    def fe2(e, nC=nC, lo=lo, W=W):
                        return e.scalar_tensor_tensor(
                            out=XR[:, nC * CW + lo: nC * CW + lo + W],
                            in0=tmpa[nC % 2][:, 0:W],
                            scalar=bqS_c[:, nC:nC + 1],
                            in1=XR[:, nC * CW + lo: nC * CW + lo + W],
                            op0=OP.add, op1=OP.add)
                    e2 = P.emit("vector", fe2, waits=[e1, ev_bqs], inc="v")
                    tmpa_guard[nC % 2] = e2
                    outs.append(e2)
                ao_done[j] = outs
                ao_prev = outs[-1]

            reload_tok = {}
            reload_last = None
            free_tok = {0: wqp_done, 1: ev_bqs, 2: ao_done[4][-1]}
            for i in range(3):
                def frl(e, i=i):
                    return e.dma_start(
                        out=WB[i][:],
                        in_=wspill.ap()[i * 128:(i + 1) * 128, :])
                reload_last = P.emit("sync", frl,
                                     waits=[free_tok[i], spill_all],
                                     inc="drl")
                reload_tok[i] = reload_last
            if KDBG:
                P.emit("sync", lambda e: e.dma_start(out=dbg_at, in_=XR[:, 0:CW]),
                       waits=[ao_done[j][-1] for j in range(5)], inc="ddbg")
                P.emit("sync", lambda e: e.dma_start(out=dbg_s, in_=srow[:]),
                       waits=[ev_sb], inc="ddbg")
            b2 = P.barrier()

        ao_chunks_p4 = [(1, 513), (513, 1025), (1025, 1537), (1537, 2049),
                        (2049, 2050)]

        def xr_wait_p4(lo, hi):
            toks = []
            for j, (clo, chi) in enumerate(ao_chunks_p4):
                if lo < chi and hi > clo:
                    toks.extend(ao_done[j])
            return toks

        ffn2_chunks = [(2, 514), (514, 1026), (1026, 1538), (1538, 2050)]
        ffn2_out = {}
        ffn((1, TL), ffn2_chunks, True, xr_wait_p4, b2 + [reload_last],
            {"r": reload_tok[0], "k": reload_tok[1], "w": reload_tok[2]},
            ffn2_out, "2")

        fin = [(k, v) for k, v in P.cnt.items()
               if k.startswith("d") and v > 0]
        P.emit("sync", lambda e: e.nop(), waits=fin)
        P.emit("vector", lambda e: e.nop(), waits=fin)

        with nc.Block() as block:
            P.flush(block)

    return nc


_NC = None


def kernel(L0_attn_x, L0_attn_Wq, L0_attn_bq, L0_attn_Wk, L0_attn_bk,
           L0_attn_Wv, L0_attn_bv, L0_ffn_W_r, L0_ffn_W_k, L0_ffn_W_v,
           L0_ffn_mu_r, L0_ffn_mu_k):
    global _NC
    if _NC is None:
        _NC = build_nc()
    x = np.ascontiguousarray(np.asarray(L0_attn_x, np.float32))
    B, T, Dd = x.shape
    pad = np.zeros((2, Dd), np.float32)
    common = {
        "Wq": np.ascontiguousarray(np.asarray(L0_attn_Wq, np.float32)),
        "Wk": np.ascontiguousarray(np.asarray(L0_attn_Wk, np.float32)),
        "Wv": np.ascontiguousarray(np.asarray(L0_attn_Wv, np.float32)),
        "bq": np.ascontiguousarray(np.asarray(L0_attn_bq, np.float32).reshape(1, -1)),
        "bk": np.ascontiguousarray(np.asarray(L0_attn_bk, np.float32).reshape(1, -1)),
        "bv": np.ascontiguousarray(np.asarray(L0_attn_bv, np.float32).reshape(1, -1)),
        "Wr": np.ascontiguousarray(np.asarray(L0_ffn_W_r, np.float32)),
        "Wkf": np.ascontiguousarray(np.asarray(L0_ffn_W_k, np.float32)),
        "Wvf": np.ascontiguousarray(np.asarray(L0_ffn_W_v, np.float32)),
        "mur": np.ascontiguousarray(np.asarray(L0_ffn_mu_r, np.float32).reshape(1, -1)),
        "muk": np.ascontiguousarray(np.asarray(L0_ffn_mu_k, np.float32).reshape(1, -1)),
    }
    in_maps = []
    for c in range(8):
        b, h = c // 2, c % 2
        if h == 0:
            shard = np.concatenate([pad, x[b, 0:2048]], axis=0)
        else:
            shard = np.ascontiguousarray(x[b, 2046:4096])
        d = dict(common)
        d["xs"] = shard
        d["maskc"] = np.full((128, 1), float(h), np.float32)
        d["identin"] = np.eye(128, dtype=np.float32)
        in_maps.append(d)
    res = run_bass_kernel_spmd(_NC, in_maps, core_ids=list(range(8)))
    kernel.last = res.results
    out = np.empty((B, T, Dd), np.float32)
    for c in range(8):
        b, h = c // 2, c % 2
        out[b, h * 2048:(h + 1) * 2048] = res.results[c]["y"]
    return out


# version 60
# speedup vs baseline: 1.0151x; 1.0151x over previous
import contextlib
import numpy as np
import concourse.bass as bass
import concourse.mybir as mybir
from concourse.bass_utils import run_bass_kernel_spmd

F32 = mybir.dt.float32
BF16 = mybir.dt.bfloat16
AF = mybir.ActivationFunctionType
OP = mybir.AluOpType

D = 1024
NT = 8
TL = 2050
CW = 2052
EPS = 1e-5

OWNER = {"t": "tensor", "v": "vector", "s": "scalar", "g": "gpsimd"}


class Prog:

    def __init__(self, nc, sems):
        self.nc = nc
        self.sems = sems
        self.cnt = {k: 0 for k in sems}
        self.items = []

    def emit(self, eng, fn, waits=(), inc=None, amt=None):
        w = []
        for tk in waits:
            if tk is None:
                continue
            if isinstance(tk, list):
                w.extend(t for t in tk if t is not None)
            else:
                w.append(tk)
        for _t in w:
            assert isinstance(_t, tuple) and len(_t) == 2, f"bad token {_t}"
        tok = None
        if inc is not None:
            if amt is None:
                amt = 16 if inc.startswith("d") else 1
            self.cnt[inc] += amt
            tok = (inc, self.cnt[inc])
        self.items.append((eng, fn, tuple(w), inc, amt))
        return tok

    def barrier(self):
        toks = []
        dw = [(k, v) for k, v in self.cnt.items()
              if k.startswith("d") and v > 0]
        toks.append(self.emit("tensor", lambda e: e.drain(fusable=False),
                              waits=dw, inc="t"))
        toks.append(self.emit("vector", lambda e: e.drain(fusable=False),
                              waits=dw, inc="v"))
        toks.append(self.emit("scalar", lambda e: e.drain(fusable=False),
                              waits=dw, inc="s"))
        toks.extend(dw)
        return toks

    def flush(self, block):
        per = {}
        for it in self.items:
            per.setdefault(it[0], []).append(it)
        for engname in ("sync", "gpsimd", "vector", "scalar", "tensor"):
            items = per.get(engname)
            if not items:
                continue

            def body(sec, items=items, engname=engname):
                water = {}
                own = {}
                pending_inc = False
                for (_eng, fn, waits, inc, amt) in items:
                    need = {}
                    for (snm, v) in waits:
                        if water.get(snm, 0) >= v:
                            continue
                        if need.get(snm, 0) < v:
                            need[snm] = v
                    if inc is not None and pending_inc and not need:
                        sec.wait_ge(self.sems[inc], own.get(inc, 0))
                        pending_inc = False
                    for snm, v in need.items():
                        sec.wait_ge(self.sems[snm], v)
                        water[snm] = v
                        pending_inc = False
                    r = fn(sec)
                    if inc is not None:
                        if r is None:
                            r = sec.nop(nofuse=True)
                        r.then_inc(self.sems[inc], amt)
                        own[inc] = own.get(inc, 0) + amt
                        pending_inc = True
            getattr(block, engname)(body)
        self.items = []


def _cover(toks, lo, hi):
    return [t[2] for t in toks if t is not None and lo < t[1] and hi > t[0]]


def _chunks(a, b, step=512):
    out = []
    lo = a
    while lo < b:
        hi = min(lo + step, b)
        out.append((lo, hi))
        lo = hi
    return out


def build_nc():
    nc = bass.Bass("TRN2", target_bir_lowering=False)

    xs = nc.dram_tensor("xs", [TL, D], F32, kind="ExternalInput").ap()
    Wq_h = nc.dram_tensor("Wq", [D, D], F32, kind="ExternalInput").ap()
    Wk_h = nc.dram_tensor("Wk", [D, D], F32, kind="ExternalInput").ap()
    Wv_h = nc.dram_tensor("Wv", [D, D], F32, kind="ExternalInput").ap()
    bq_h = nc.dram_tensor("bq", [1, D], F32, kind="ExternalInput").ap()
    bk_h = nc.dram_tensor("bk", [1, D], F32, kind="ExternalInput").ap()
    bv_h = nc.dram_tensor("bv", [1, D], F32, kind="ExternalInput").ap()
    Wr_h = nc.dram_tensor("Wr", [D, D], F32, kind="ExternalInput").ap()
    Wkf_h = nc.dram_tensor("Wkf", [D, D], F32, kind="ExternalInput").ap()
    Wvf_h = nc.dram_tensor("Wvf", [D, D], F32, kind="ExternalInput").ap()
    mur_h = nc.dram_tensor("mur", [1, D], F32, kind="ExternalInput").ap()
    muk_h = nc.dram_tensor("muk", [1, D], F32, kind="ExternalInput").ap()
    mask_h = nc.dram_tensor("maskc", [128, 1], F32, kind="ExternalInput").ap()
    ident_h = nc.dram_tensor("identin", [128, 128], F32, kind="ExternalInput").ap()

    y_out = nc.dram_tensor("y", [2048, D], F32, kind="ExternalOutput").ap()
    import os as _os
    KDBG = bool(_os.environ.get("KDBG"))
    if KDBG:
        dbg_p0 = nc.dram_tensor("dbg_p0", [128, CW], F32, kind="ExternalOutput").ap()
        dbg_f1 = nc.dram_tensor("dbg_f1", [128, CW], F32, kind="ExternalOutput").ap()
        dbg_at = nc.dram_tensor("dbg_at", [128, CW], F32, kind="ExternalOutput").ap()
        dbg_s = nc.dram_tensor("dbg_s", [1, D], F32, kind="ExternalOutput").ap()
        dbg_wq = nc.dram_tensor("dbg_wq", [3 * 128, NT * D], mybir.dt.bfloat16,
                                kind="ExternalOutput").ap()
        dbg_bc = nc.dram_tensor("dbg_bc", [2 * 128, CW], F32,
                                kind="ExternalOutput").ap()
        dbg_cq = nc.dram_tensor("dbg_cq", [128, 24], F32,
                                kind="ExternalOutput").ap()

    Cdd = nc.dram_tensor("Cdd", [D, 2 * D], BF16)
    sdd = nc.dram_tensor("sdd", [1, 2 * D], BF16)
    wspill = nc.dram_tensor("wspill", [3 * 128, NT * D], BF16)
    wattn = nc.dram_tensor("wattn", [3 * 128, NT * D], BF16)
    brow_sc = nc.dram_tensor("brow_sc", [3, D], BF16)
    cc_in = nc.dram_tensor("cc_sin", [128, 128], F32)
    cc_out = nc.dram_tensor("cc_sout", [128, 128], F32)

    est = contextlib.ExitStack()
    with est:
        sem_names = ["dms", "dx0", "dx1", "dx2", "dx3", "dw0", "dw1",
                     "dwv0", "dwv1", "dpk0", "dpk1", "drow", "drw2",
                     "dab", "dbias", "dwA", "dwA2", "dwA3", "dspill",
                     "dsp2", "dsp3", "dwS0", "dwS1",
                     "dC0", "dC1", "dC2", "dC3", "dC4", "dC5", "dC6",
                     "dC7", "drot0", "drot1", "dsdd", "dS", "dgs", "drl",
                     "dout", "ddbg", "cc", "t", "v", "s", "g"]
        sem_names += [f"dnw{tg}{ci}" for tg in "12" for ci in range(10)]
        sems = {n: est.enter_context(nc.semaphore(n))
                for n in sem_names}
        P = Prog(nc, sems)

        XR = est.enter_context(nc.sbuf_tensor("XR", [128, NT * CW], F32))
        WB = [est.enter_context(nc.sbuf_tensor(f"WB{i}", [128, NT * D], BF16))
              for i in range(3)]
        AB_r = est.enter_context(nc.sbuf_tensor("ABr", [2, D], BF16))
        AB_k = est.enter_context(nc.sbuf_tensor("ABk", [2, D], BF16))
        identf = est.enter_context(nc.sbuf_tensor("identf", [128, 128], F32))
        identb = est.enter_context(nc.sbuf_tensor("identb", [128, 128], BF16))
        ones_row_b = est.enter_context(nc.sbuf_tensor("onesrb", [1, 128], BF16))
        negones_row_b = est.enter_context(nc.sbuf_tensor("nonesrb", [1, 128], BF16))
        ones_row_f = est.enter_context(nc.sbuf_tensor("onesrf", [1, 128], F32))
        ones_col_f = est.enter_context(nc.sbuf_tensor("onescf", [128, 1], F32))
        ones_col_b = est.enter_context(nc.sbuf_tensor("onescb", [128, 1], BF16))
        invD_col_b = est.enter_context(nc.sbuf_tensor("invDcb", [128, 1], BF16))
        mask_col = est.enter_context(nc.sbuf_tensor("maskcol", [128, 1], F32))
        mu_cols = est.enter_context(nc.sbuf_tensor("mucols", [128, 16], BF16))

        c_mask = P.emit("sync", lambda e: e.dma_start(out=mask_col[:], in_=mask_h),
                        inc="dms")
        t_id0 = P.emit("sync", lambda e: e.dma_start(out=identf[:], in_=ident_h),
                       inc="dms")
        c_mask = t_id0
        t_idb = P.emit("vector", lambda e: e.tensor_copy(out=identb[:], in_=identf[:]),
                       waits=[t_id0], inc="v")
        P.emit("vector", lambda e: e.memset(ones_row_b[:], 1.0))
        P.emit("vector", lambda e: e.memset(negones_row_b[:], -1.0))
        P.emit("vector", lambda e: e.memset(ones_row_f[:], 1.0))
        P.emit("vector", lambda e: e.memset(ones_col_f[:], 1.0))
        P.emit("vector", lambda e: e.memset(ones_col_b[:], 1.0))
        cst_eps = est.enter_context(nc.sbuf_tensor("cst_eps", [128, 1], F32))
        cst_nhalf = est.enter_context(nc.sbuf_tensor("cst_nhalf", [128, 1], F32))
        cst_half = est.enter_context(nc.sbuf_tensor("cst_half", [128, 1], F32))
        cst_neg1 = est.enter_context(nc.sbuf_tensor("cst_neg1", [128, 1], F32))
        P.emit("vector", lambda e: e.memset(cst_eps[:], float(EPS)))
        P.emit("vector", lambda e: e.memset(cst_nhalf[:], -0.5))
        P.emit("vector", lambda e: e.memset(cst_half[:], 0.5))
        P.emit("vector", lambda e: e.memset(cst_neg1[:], -1.0))
        nc.const_aps.register(F32, float(EPS), cst_eps[:])
        nc.const_aps.register(F32, -0.5, cst_nhalf[:])
        nc.const_aps.register(F32, 0.5, cst_half[:])
        nc.const_aps.register(F32, -1.0, cst_neg1[:])
        t_consts = P.emit("vector", lambda e: e.memset(invD_col_b[:], 1.0 / D),
                          inc="v")

        state = {}
        park = est.enter_context(nc.sbuf_tensor("park", [128, 2 * D], BF16))

        pst = contextlib.ExitStack()
        with pst:
            stage = pst.enter_context(nc.sbuf_tensor("stage", [128, NT * D], F32))
            wnat0 = pst.enter_context(nc.sbuf_tensor("wnat0", [128, NT * D], BF16))
            wnat1 = pst.enter_context(nc.sbuf_tensor("wnat1", [128, NT * D], BF16))
            wnat2 = pst.enter_context(nc.sbuf_tensor("wnat2", [128, NT * D], BF16))
            prow = pst.enter_context(nc.sbuf_tensor("prow", [1, D], BF16))
            prow2 = pst.enter_context(nc.sbuf_tensor("prow2", [1, D], BF16))
            ps_tr0 = pst.enter_context(nc.psum_tensor("pstr0", [128, 512], F32))
            ps_tr1 = pst.enter_context(nc.psum_tensor("pstr1", [128, 512], F32))
            ps_wtr0 = pst.enter_context(nc.psum_tensor("pswt0", [128, 512], BF16))
            ps_wtr1 = pst.enter_context(nc.psum_tensor("pswt1", [128, 512], BF16))
            ps_muc = pst.enter_context(nc.psum_tensor("psmuc", [128, 32], BF16))
            ps_abA = pst.enter_context(nc.psum_tensor("psabA", [1, 512], F32))
            ps_abB = pst.enter_context(nc.psum_tensor("psabB", [1, 512], F32))
            rowB = prow2
            def row_cast(src_h, dstrow, guard):
                return P.emit("gpsimd", lambda e, src_h=src_h, dstrow=dstrow:
                              e.dma_start(out=dstrow[:], in_=src_h),
                              waits=list(guard), inc="drow")

            cw_mur = row_cast(mur_h, prow, [])
            cw_muk = row_cast(muk_h, prow2, [])
            for half16 in range(2):
                rowt = prow if half16 == 0 else prow2
                cw = cw_mur if half16 == 0 else cw_muk
                for j8 in range(8):
                    j = half16 * 8 + j8

                    def ft(e, j=j, j8=j8, rowt=rowt):
                        return e.transpose(
                            out=ps_muc[:, 2 * j:2 * j + 1],
                            in_=rowt[0:1, j8 * 128:j8 * 128 + 128],
                            identity=identb[0:1, 0:1])
                    mu_tk = P.emit("tensor", ft, waits=[cw, t_idb],
                                   inc="t" if j8 == 7 else None)
            muc_ev = P.emit("vector", lambda e: e.tensor_copy(
                out=mu_cols[:], in_=ps_muc[:, 0:32:2]), waits=[mu_tk], inc="v")

            ps_tr = [ps_tr0, ps_tr1]
            ld = {}
            grp_ev = {}
            bank_guard = {}
            bidx = 0
            for g in range(4):
                for k in range(4):
                    i = g * 4 + k
                    slot = i % 8
                    w = grp_ev.get(g - 2, [])

                    def f(e, i=i, slot=slot):
                        return e.dma_start(
                            out=stage[:, slot * D:(slot + 1) * D],
                            in_=xs[i * 128:(i + 1) * 128, :])
                    ld[i] = P.emit("sync", f, waits=w,
                                   inc="dx%d" % (g % 4))
                evs = []
                for cT in range(8):
                    bank = ps_tr[bidx % 2]
                    for k in range(4):
                        slot = (g * 4 + k) % 8

                        def f(e, slot=slot, cT=cT, k=k, bank=bank):
                            return e.transpose(
                                out=bank[:, k * 128:(k + 1) * 128],
                                in_=stage[:, slot * D + cT * 128:
                                          slot * D + cT * 128 + 128],
                                identity=identf[:])
                        w = [ld[g * 4 + kk] for kk in range(4)]
                        w += [t_id0, bank_guard.get(bidx % 2)]
                        tk = P.emit("tensor", f, waits=w,
                                    inc="t" if k == 3 else None)

                    def fe(e, cT=cT, g=g, bank=bank):
                        return e.tensor_copy(
                            out=XR[:, cT * CW + g * 512: cT * CW + g * 512 + 512],
                            in_=bank[:])
                    ev = P.emit("vector", fe, waits=[tk], inc="v")
                    bank_guard[bidx % 2] = ev
                    evs.append(ev)
                    bidx += 1
                grp_ev[g] = evs
            ld16 = P.emit("sync", lambda e: e.dma_start(
                out=stage[0:2, 0:D], in_=xs[2048:2050, :]),
                waits=grp_ev.get(2, []), inc="dx0")
            tail_evs = []
            for cT in range(8):
                bank = ps_tr[bidx % 2]

                def f(e, cT=cT, bank=bank):
                    return e.transpose(
                        out=bank[:, 0:2],
                        in_=stage[0:2, cT * 128:cT * 128 + 128],
                        identity=identf[0:2, 0:2])
                tk = P.emit("tensor", f,
                            waits=[ld16, t_id0, bank_guard.get(bidx % 2)],
                            inc="t")

                def fe(e, cT=cT, bank=bank):
                    return e.tensor_copy(
                        out=XR[:, cT * CW + 2048: cT * CW + 2050],
                        in_=bank[:, 0:2])
                ev = P.emit("vector", fe, waits=[tk], inc="v")
                bank_guard[bidx % 2] = ev
                tail_evs.append(ev)
                bidx += 1
            xr_ready_g = grp_ev
            xr_tail = tail_evs

            wnat = [wnat0, wnat1, wnat2]
            ps_wtr = [ps_wtr0, ps_wtr1]
            w_srcs = [Wr_h, Wkf_h, Wvf_h]
            w_chs = [("dw0", "dw1"), ("dwS0", "dwS1"), ("dwv0", "dwv1")]
            wbank_guard = {}
            cast_by = []
            for wi in range(3):
                toks = []
                for half in range(2):
                    def fl(e, src=w_srcs[wi], half=half, nat=wnat[wi]):
                        return e.dma_start(
                            out=nat[:, half * 4 * D:(half + 1) * 4 * D]
                                .rearrange("p (dt c) -> p dt c", dt=4),
                            in_=bass.AP(src.tensor, half * 4 * 128 * D,
                                        [[D, 128], [128 * D, 4], [1, D]]))
                    toks.append(P.emit("gpsimd", fl, inc=w_chs[wi][half]))
                cast_by.append(toks)
            wt_slot = []
            for wi in range(3):
                nat = wnat[wi]
                ev = None
                for cT in range(8):
                    for half in range(2):
                        par = (cT * 2 + half) % 2
                        bank = ps_wtr[par]
                        for j in range(4):
                            dT = half * 4 + j

                            def ft(e, nat=nat, dT=dT, cT=cT, j=j, bank=bank):
                                return e.transpose(
                                    out=bank[:, j * 128:(j + 1) * 128],
                                    in_=nat[:, dT * D + cT * 128:
                                            dT * D + cT * 128 + 128],
                                    identity=identb[:])
                            w = [cast_by[wi][half], t_idb,
                                 wbank_guard.get(par)]
                            tk = P.emit("tensor", ft, waits=w,
                                        inc="t" if j == 3 else None)

                        def fe(e, wi=wi, cT=cT, half=half, bank=bank):
                            if wi == 2:
                                return e.activation(
                                    out=WB[wi][:, cT * D + half * 512:
                                               cT * D + half * 512 + 512],
                                    in_=bank[:], func=AF.Copy, scale=0.5)
                            return e.copy(
                                out=WB[wi][:, cT * D + half * 512:
                                           cT * D + half * 512 + 512],
                                in_=bank[:])
                        ev = P.emit("scalar", fe, waits=[tk, t_consts], inc="s")
                        wbank_guard[par] = ev
                wt_slot.append(ev)
            wt_ready_ffn1 = {"r": wt_slot[0], "k": wt_slot[1],
                             "w": wt_slot[2]}



            prevb = mu_tk
            for bi, bh in ((0, bk_h), (1, bv_h), (2, bq_h)):
                bt = row_cast(bh, prow, [prevb])
                prevb = P.emit("sync", lambda e, bi=bi: e.dma_start(
                    out=brow_sc.ap()[bi:bi + 1, :], in_=prow[:]),
                    waits=[bt], inc="drw2")
            brow_done = prevb


            ctmp = prow
            ab_tok = {}
            bankA_last = bankB_last = None
            evB = None
            dB_prev = None
            for (ABt, mu_off, wb, wtok, pn) in (
                    (AB_r, 0, WB[0], wt_slot[0], "r"),
                    (AB_k, 8, WB[1], wt_slot[1], "k")):
                for nh in range(2):
                    for cT in range(8):
                        def fm(e, mu_off=mu_off, cT=cT, wb=wb, nh=nh):
                            return e.matmul(
                                out=ps_abA[0:1, :],
                                lhsT=mu_cols[:, mu_off + cT:mu_off + cT + 1],
                                rhs=wb[:, cT * D + nh * 512:
                                       cT * D + nh * 512 + 512],
                                start=(cT == 0), stop=(cT == 7))
                        tka = P.emit("tensor", fm,
                                     waits=[muc_ev, wtok, bankA_last],
                                     inc="t" if cT == 7 else None)

                        def fc(e, cT=cT, wb=wb, nh=nh):
                            return e.matmul(
                                out=ps_abB[0:1, :],
                                lhsT=ones_col_b[:],
                                rhs=wb[:, cT * D + nh * 512:
                                       cT * D + nh * 512 + 512],
                                start=(cT == 0), stop=(cT == 7))
                        tkc = P.emit("tensor", fc,
                                     waits=[t_consts, bankB_last],
                                     inc="t" if cT == 7 else None)
                    evA = P.emit("scalar", lambda e, ABt=ABt, nh=nh: e.copy(
                        out=ABt[0:1, nh * 512:nh * 512 + 512],
                        in_=ps_abA[0:1, :]),
                        waits=[tka], inc="s")
                    bankA_last = evA
                    evC = P.emit("scalar", lambda e, nh=nh: e.copy(
                        out=ctmp[0:1, nh * 512:nh * 512 + 512],
                        in_=ps_abB[0:1, :]),
                        waits=[tkc, brow_done, evB], inc="s")
                    bankB_last = evC
                    evB = P.emit("gpsimd", lambda e, ABt=ABt, nh=nh:
                                 e.tensor_sub(
                                     out=rowB[0:1, nh * 512:nh * 512 + 512],
                                     in0=ctmp[0:1, nh * 512:nh * 512 + 512],
                                     in1=ABt[0:1, nh * 512:nh * 512 + 512]),
                                 waits=[evC, evA, dB_prev], inc="g")
                dB = P.emit("sync", lambda e, ABt=ABt: e.dma_start(
                    out=ABt[1:2, :], in_=rowB[:]), waits=[evB], inc="dab")
                dB_prev = dB
                ab_tok[pn] = dB
            ab_ready = dB

            p0_join = P.barrier()

        state["p0_join"] = p0_join
        if KDBG:
            P.emit("sync", lambda e: e.dma_start(out=dbg_p0, in_=XR[:, 0:CW]),
                   waits=p0_join, inc="ddbg")

        def make_stats(scope, rng_lo, rng_hi, inv_row, negw_dst=None,
                       negminv_bf=None, wait_cols=None, entry=(), shift_ch=None,
                       bc_dst=None, bc_psum=None):
            (cbuf, sqtmp, rs1, rs2, rs3, ps_mu) = scope
            chunks = _chunks(rng_lo, rng_hi, 256)
            toks = [None] * len(chunks)
            state = {"guard": list(entry), "bcg": None, "sqg": None}

            def emit_chunk(ci):
                lo, hi = chunks[ci]
                W = hi - lo
                wz = (wait_cols(lo, hi) if wait_cols else []) + list(entry)
                sq_wz = wz + [state["sqg"]]
                for t4 in range(4):
                    def fa(e, t4=t4, lo=lo, W=W):
                        return e.tensor_add(
                            out=cbuf[:, t4 * 512:t4 * 512 + W],
                            in0=XR[:, 2 * t4 * CW + lo:
                                   2 * t4 * CW + lo + W],
                            in1=XR[:, (2 * t4 + 1) * CW + lo:
                                   (2 * t4 + 1) * CW + lo + W])
                    tc = P.emit("vector", fa, waits=wz,
                                inc="v" if t4 == 3 else None)
                    for h2 in range(2):
                        def fs(e, t4=t4, h2=h2, lo=lo, W=W):
                            return e.activation(
                                out=sqtmp[:, (2 * t4 + h2) * 256:
                                          (2 * t4 + h2) * 256 + W],
                                in_=XR[:, (2 * t4 + h2) * CW + lo:
                                       (2 * t4 + h2) * CW + lo + W],
                                func=AF.Square)
                        ts_ = P.emit("scalar", fs, waits=sq_wz,
                                     inc="s" if h2 == 1 else None)

                    def fq(e, t4=t4, W=W):
                        return e.tensor_add(
                            out=cbuf[:, t4 * 512 + 256:t4 * 512 + 256 + W],
                            in0=sqtmp[:, 2 * t4 * 256:2 * t4 * 256 + W],
                            in1=sqtmp[:, (2 * t4 + 1) * 256:
                                      (2 * t4 + 1) * 256 + W])
                    tq = P.emit("vector", fq, waits=[ts_],
                                inc="v" if t4 == 3 else None)
                state["sqg"] = tq
                cast_toks = [tc, tq]
                for t4 in range(4):
                    def fm(e, t4=t4):
                        return e.matmul(out=ps_mu[0:1, 0:512],
                                        lhsT=invD_col_b[:],
                                        rhs=cbuf[:, t4 * 512:t4 * 512 + 512],
                                        start=(t4 == 0), stop=(t4 == 3))
                    t_mu = P.emit("tensor", fm,
                                  waits=(cast_toks + [t_consts]
                                         + state["guard"]) if t4 == 0 else [],
                                  inc="t" if t4 == 3 else None)
                e1 = P.emit("vector", lambda e, W=W: e.tensor_copy(
                    out=rs1[0:1, 0:W], in_=ps_mu[0:1, 0:W]), waits=[t_mu],
                    inc="v")
                e2 = P.emit("scalar", lambda e, W=W: e.copy(
                    out=rs2[0:1, 0:W], in_=ps_mu[0:1, 256:256 + W]),
                    waits=[t_mu, e1], inc="s")
                state["guard"] = [e1, e2]
                e3 = P.emit("vector", lambda e, W=W: e.tensor_mul(
                    out=rs3[0:1, 0:W], in0=rs1[0:1, 0:W], in1=rs1[0:1, 0:W]),
                    waits=[e1], inc="v")
                e4 = P.emit("vector", lambda e, W=W: e.tensor_sub(
                    out=rs2[0:1, 0:W], in0=rs2[0:1, 0:W], in1=rs3[0:1, 0:W]),
                    waits=[e2, e3], inc="v")
                e5 = P.emit("scalar", lambda e, W=W: e.activation(
                    out=rs3[0:1, 0:W], in_=rs2[0:1, 0:W], func=AF.Ln,
                    bias=float(EPS)), waits=[e4, t_consts], inc="s")
                e6 = P.emit("scalar", lambda e, W=W, lo=lo: e.activation(
                    out=inv_row[0:1, lo:lo + W], in_=rs3[0:1, 0:W], func=AF.Exp,
                    scale=-0.5), waits=[e5, t_consts], inc="s")
                last = e6
                if bc_dst is not None:
                    tbc = P.emit("tensor", lambda e, W=W, lo=lo: e.matmul(
                        out=bc_psum[:, 0:W], lhsT=ones_row_f[:],
                        rhs=inv_row[0:1, lo:lo + W], start=True, stop=True),
                        waits=[e6, state["bcg"]], inc="t")
                    last = P.emit("vector", lambda e, W=W, lo=lo: e.tensor_copy(
                        out=bc_dst[:, lo:lo + W], in_=bc_psum[:, 0:W]),
                        waits=[tbc], inc="v")
                    state["bcg"] = last
                if negw_dst is not None:
                    last = P.emit("vector",
                                  lambda e, W=W, lo=lo, d=negw_dst:
                                  e.scalar_tensor_tensor(
                                      out=d[0:1, lo:lo + W],
                                      in0=rs1[0:1, 0:W], scalar=-1.0,
                                      in1=inv_row[0:1, lo:lo + W],
                                      op0=OP.mult, op1=OP.mult),
                                  waits=[e6, t_consts], inc="v")
                    if lo <= 1 < hi:
                        last = P.emit("vector", lambda e, d=negw_dst:
                                      e.tensor_scalar(
                                          out=d[0:1, 1:2], in0=d[0:1, 1:2],
                                          scalar1=mask_col[0:1, :],
                                          scalar2=None, op0=OP.mult),
                                      waits=[last, c_mask], inc="v")
                    last = P.emit("sync", lambda e, d=negw_dst, lo=lo, W=W:
                                  e.dma_start(out=d[1:2, lo + 1:lo + 1 + W],
                                              in_=d[0:1, lo:lo + W]),
                                  waits=[last], inc=shift_ch + str(ci))
                if negminv_bf is not None:
                    last = P.emit("vector",
                                  lambda e, W=W, lo=lo, d=negminv_bf:
                                  e.scalar_tensor_tensor(
                                      out=d[0:1, lo:lo + W],
                                      in0=rs1[0:1, 0:W], scalar=-1.0,
                                      in1=inv_row[0:1, lo:lo + W],
                                      op0=OP.mult, op1=OP.mult),
                                  waits=[e6, t_consts], inc="v")
                toks[ci] = (lo, hi, last)
            return chunks, emit_chunk, toks

        def ffn(ln_rng, out_chunks, is_ffn2, xr_wait_fn, entry, wt_tok,
                out_record, tag):
            fst = contextlib.ExitStack()
            with fst:
                inv_row = fst.enter_context(nc.sbuf_tensor("f_inv" + tag, [1, CW], F32))
                negw2b = fst.enter_context(nc.sbuf_tensor("f_negwb" + tag, [2, CW + 4], BF16))
                rs1 = fst.enter_context(nc.sbuf_tensor("f_rs1" + tag, [1, 512], F32))
                rs2 = fst.enter_context(nc.sbuf_tensor("f_rs2" + tag, [1, 512], F32))
                rs3 = fst.enter_context(nc.sbuf_tensor("f_rs3" + tag, [1, 512], F32))
                ps_r0 = fst.enter_context(nc.psum_tensor("f_pr0" + tag, [128, 512], F32))
                ps_r1 = fst.enter_context(nc.psum_tensor("f_pr1" + tag, [128, 512], F32))
                ps_k0 = fst.enter_context(nc.psum_tensor("f_pk0" + tag, [128, 512], F32))
                ps_k1 = fst.enter_context(nc.psum_tensor("f_pk1" + tag, [128, 512], F32))
                ps_w0 = fst.enter_context(nc.psum_tensor("f_pw0" + tag, [128, 512], F32))
                ps_w1 = fst.enter_context(nc.psum_tensor("f_pw1" + tag, [128, 512], F32))
                ps_mu = fst.enter_context(nc.psum_tensor("f_pmu" + tag, [1, 512], F32))
                cbuf = fst.enter_context(nc.sbuf_tensor("f_cb" + tag, [128, 4 * 512], BF16))
                sqtmp = fst.enter_context(nc.sbuf_tensor("f_sq" + tag, [128, 8 * 256], BF16))
                inv128 = fst.enter_context(nc.sbuf_tensor("f_inv128" + tag, [128, CW], BF16))
                ps_bc = fst.enter_context(nc.psum_tensor("f_pbc" + tag, [128, 256], F32))
                ubuf = fst.enter_context(nc.sbuf_tensor("f_u" + tag, [128, NT * 516], BF16))
                dbuf = fst.enter_context(nc.sbuf_tensor("f_d" + tag, [128, NT * 512], BF16))
                xrb = fst.enter_context(nc.sbuf_tensor("f_xr" + tag, [128, NT * 512], BF16))
                xkb = fst.enter_context(nc.sbuf_tensor("f_xk" + tag, [128, NT * 512], BF16))
                rbuf = fst.enter_context(nc.sbuf_tensor("f_r" + tag, [128, NT * 512], BF16))
                kact = fst.enter_context(nc.sbuf_tensor("f_kact" + tag, [128, NT * 512], BF16))
                tmpv = fst.enter_context(nc.sbuf_tensor("f_tmpv" + tag, [128, 512], F32))
                krelu = tmpv
                natst = fst.enter_context(nc.sbuf_tensor(
                    "f_nat" + tag, [128, (D if is_ffn2 else 1)], F32))
                mz = P.emit("vector",
                            lambda e: e.memset(negw2b[0:2, 0:1], 0.0),
                            waits=list(entry), inc="v")
                scope = (cbuf, sqtmp, rs1, rs2, rs3, ps_mu)
                st_chunks, st_emit, st_toks = make_stats(
                    scope, ln_rng[0], ln_rng[1], inv_row, negw_dst=negw2b,
                    wait_cols=xr_wait_fn, entry=list(entry) + [mz],
                    shift_ch="dnw" + tag, bc_dst=inv128, bc_psum=ps_bc)
                st_n = [0]

                def st_need(hi_col):
                    while st_n[0] < len(st_chunks) and                             st_chunks[st_n[0]][0] < hi_col:
                        st_emit(st_n[0])
                        st_n[0] += 1

                ps_r = [ps_r0, ps_r1]
                ps_k = [ps_k0, ps_k1]
                wb4 = [ps_w0, ps_w1, ps_k0, ps_k1]
                last_touch = {}
                mix_done_by = {}
                d_by = {}
                state = {"prev_u": None, "prev_ureaders": []}
                natst_guard = [None]
                import os as _os
                if _os.environ.get("NO_ILV"):
                    st_need(10 ** 9)

                def prep(isc):
                    lo, hi = out_chunks[isc]
                    W = hi - lo
                    wlo = lo - 1
                    WW = hi - wlo
                    par = isc % 2
                    st_need(hi + 1)
                    u_toks = []
                    if isc == 0:
                        for cT in range(8):
                            def fu(e, cT=cT, wlo=wlo, WW=WW):
                                return e.tensor_mul(
                                    out=ubuf[:, cT * 516:cT * 516 + WW],
                                    in0=XR[:, cT * CW + wlo:
                                           cT * CW + wlo + WW],
                                    in1=inv128[:, wlo:wlo + WW])
                            wz = (xr_wait_fn(wlo, hi) if xr_wait_fn else []) \
                                + entry
                            u_toks.append(P.emit(
                                "vector", fu,
                                waits=_cover(st_toks, wlo, hi) + wz,
                                inc="v" if cT == 7 else None))
                    else:
                        off = lo - out_chunks[isc - 1][0]
                        cpy = P.emit("vector", lambda e, off=off:
                                     e.tensor_copy(
                                         out=ubuf[:, 0:7 * 516 + 1:516],
                                         in_=ubuf[:, off:off + 7 * 516 + 1:
                                                  516]),
                                     waits=[state["prev_u"]]
                                     + state["prev_ureaders"], inc="v")
                        for cT in range(8):
                            def fu(e, cT=cT, lo=lo, W=W):
                                return e.tensor_mul(
                                    out=ubuf[:, cT * 516 + 1:
                                             cT * 516 + 1 + W],
                                    in0=XR[:, cT * CW + lo: cT * CW + lo + W],
                                    in1=inv128[:, lo:lo + W])
                            wz = (xr_wait_fn(lo, hi) if xr_wait_fn else []) \
                                + entry
                            u_toks.append(P.emit(
                                "vector", fu,
                                waits=_cover(st_toks, wlo, hi) + wz + [cpy],
                                inc="v" if cT == 7 else None))
                    u_tok = u_toks[-1]
                    state["prev_u"] = u_tok
                    if wlo <= 1 < hi:
                        idx = 1 - wlo
                        for cT in range(8):
                            def fm(e, cT=cT, idx=idx):
                                return e.tensor_scalar(
                                    out=ubuf[:, cT * 516 + idx:
                                             cT * 516 + idx + 1],
                                    in0=ubuf[:, cT * 516 + idx:
                                             cT * 516 + idx + 1],
                                    scalar1=mask_col[:], scalar2=None,
                                    op0=OP.mult)
                            u_tok = P.emit("vector", fm,
                                           waits=[u_tok, c_mask],
                                           inc="v" if cT == 7 else None)
                    for cT in range(8):
                        def fd(e, cT=cT, W=W):
                            return e.tensor_sub(
                                out=dbuf[:, cT * 512:cT * 512 + W],
                                in0=ubuf[:, cT * 516 + 1:cT * 516 + 1 + W],
                                in1=ubuf[:, cT * 516:cT * 516 + W])
                        d_tok = P.emit("vector", fd, waits=[u_tok],
                                       inc="v" if cT == 7 else None)
                    d_by[isc] = d_tok
                    state["prev_ureaders"] = [d_tok]
                    if isc == 0:
                        emit_mix(0, "r", None)
                        emit_mix(0, "k", None)

                def emit_mix(isc, key, extra):
                    lo, hi = out_chunks[isc]
                    W = hi - lo
                    dst, mu_off = (xrb, 0) if key == "r" else (xkb, 8)
                    for cT in range(8):
                        def fx(e, dst=dst, cT=cT, mu_off=mu_off, W=W):
                            return e.scalar_tensor_tensor(
                                out=dst[:, cT * 512:cT * 512 + W],
                                in0=dbuf[:, cT * 512:cT * 512 + W],
                                scalar=mu_cols[:, mu_off + cT:
                                               mu_off + cT + 1],
                                in1=ubuf[:, cT * 516:cT * 516 + W],
                                op0=OP.mult, op1=OP.add)
                        mt = P.emit("vector", fx,
                                    waits=[d_by[isc], muc_ev, extra],
                                    inc="v" if cT == 7 else None)
                    mix_done_by.setdefault(isc, {})[key] = mt
                    state["prev_ureaders"].append(mt)

                def compute(isc):
                    lo, hi = out_chunks[isc]
                    W = hi - lo
                    par = isc % 2
                    mix_done = mix_done_by[isc]
                    prev_out = last_touch.get(("outlast", isc - 1))
                    evac_sig = {}
                    for (pname, psl, wslot, ABt) in (("r", ps_r, WB[0], AB_r),
                                                     ("k", ps_k, WB[1], AB_k)):
                        for dC in range(8):
                            bank = psl[dC % 2]
                            for cT in range(8):
                                def fm(e, wslot=wslot, cT=cT, dC=dC, bank=bank,
                                       pname=pname, W=W):
                                    rhs = xrb if pname == "r" else xkb
                                    return e.matmul(
                                        out=bank[:, 0:W],
                                        lhsT=wslot[:, cT * D + dC * 128:
                                                   cT * D + dC * 128 + 128],
                                        rhs=rhs[:, cT * 512:cT * 512 + W],
                                        start=(cT == 0), stop=False)
                                w = ([mix_done[pname],
                                      last_touch.get((pname, dC % 2)),
                                      wt_tok[pname], ab_tok[pname]]
                                     + _cover(st_toks, lo - 1, hi)
                                     if cT == 0 else [])
                                P.emit("tensor", fm, waits=w)

                            def fk2(e, ABt=ABt, dC=dC, bank=bank, lo=lo, W=W):
                                return e.matmul(
                                    out=bank[:, 0:W],
                                    lhsT=ABt[0:2, dC * 128:dC * 128 + 128],
                                    rhs=negw2b[0:2, lo:lo + W],
                                    start=False, stop=True)
                            tk = P.emit("tensor", fk2, inc="t")
                            if pname == "r":
                                def fe(e, dC=dC, bank=bank, W=W):
                                    return e.activation(
                                        out=rbuf[:, dC * 512:dC * 512 + W],
                                        in_=bank[:, 0:W], func=AF.Sigmoid)
                                ev = P.emit("scalar", fe,
                                            waits=[tk, last_touch.get(
                                                ("rbuf", dC))], inc="s")
                            else:
                                def fr(e, bank=bank, W=W):
                                    return e.activation(
                                        out=krelu[:, 0:W], in_=bank[:, 0:W],
                                        func=AF.Relu)
                                ev0 = P.emit("scalar", fr,
                                             waits=[tk, prev_out,
                                                    evac_sig.get(("k",
                                                                  dC - 1))],
                                             inc="s")

                                def fs(e, dC=dC, W=W):
                                    return e.activation(
                                        out=kact[:, dC * 512:dC * 512 + W],
                                        in_=krelu[:, 0:W], func=AF.Square)
                                ev = P.emit("scalar", fs, waits=[ev0], inc="s")
                            last_touch[(pname, dC % 2)] = ev
                            evac_sig[(pname, dC)] = ev
                        last_touch[("mmdone", pname, isc)] = \
                            evac_sig[(pname, 7)]
                        if isc + 1 < len(out_chunks):
                            emit_mix(isc + 1, pname, evac_sig[(pname, 7)])
                    if isc + 2 < len(out_chunks):
                        st_need(out_chunks[isc + 2][1] + 1)

                    out_sig = []
                    for eC in range(8):
                        bank = wb4[eC % 4]
                        for dT in range(8):
                            def fm(e, dT=dT, eC=eC, bank=bank, W=W):
                                return e.matmul(
                                    out=bank[:, 0:W],
                                    lhsT=WB[2][:, dT * D + eC * 128:
                                               dT * D + eC * 128 + 128],
                                    rhs=kact[:, dT * 512:dT * 512 + W],
                                    start=(dT == 0), stop=(dT == 7))
                            w = [] if dT else [evac_sig[("k", 7)],
                                               last_touch.get(("w", eC % 4)),
                                               last_touch.get(("k", eC % 2))
                                               if eC % 4 >= 2 else None,
                                               wt_tok["w"]]
                            tk = P.emit("tensor", fm, waits=w,
                                        inc="t" if dT == 7 else None)

                        def fmul(e, bank=bank, eC=eC, W=W):
                            return e.tensor_mul(
                                out=tmpv[:, 0:W], in0=bank[:, 0:W],
                                in1=rbuf[:, eC * 512:eC * 512 + W])
                        ev0 = P.emit("vector", fmul,
                                     waits=[tk, evac_sig[("r", eC)],
                                            out_sig[-1] if out_sig else None],
                                     inc="v")
                        last_touch[("rbuf", eC)] = ev0
                        last_touch[("w", eC % 4)] = ev0
                        if eC % 4 >= 2:
                            last_touch[("k", eC % 2)] = ev0

                        def fadd(e, eC=eC, lo=lo, W=W):
                            return e.tensor_add(
                                out=XR[:, eC * CW + lo: eC * CW + lo + W],
                                in0=XR[:, eC * CW + lo: eC * CW + lo + W],
                                in1=tmpv[:, 0:W])
                        out_sig.append(P.emit("vector", fadd, waits=[ev0],
                                              inc="v"))
                    out_record[isc] = list(out_sig)
                    last_touch[("outlast", isc)] = out_sig[-1]

                    if is_ffn2:
                        for tT in range(4):
                            evns = []
                            for g2 in range(2):
                                rpar = (tT * 2 + g2) % 2
                                bank = ps_r[rpar]
                                for k in range(4):
                                    cT = g2 * 4 + k

                                    def ft(e, cT=cT, tT=tT, k=k, bank=bank,
                                           lo=lo):
                                        return e.transpose(
                                            out=bank[:, k * 128:(k + 1) * 128],
                                            in_=XR[:, cT * CW + lo + tT * 128:
                                                   cT * CW + lo + tT * 128
                                                   + 128],
                                            identity=identf[:])
                                    w = [out_sig[cT],
                                         last_touch.get(("r", rpar))]
                                    tk = P.emit("tensor", ft, waits=w,
                                                inc="t" if k == 3 else None)

                                def fe(e, g2=g2, bank=bank):
                                    return e.tensor_copy(
                                        out=natst[:, g2 * 512:g2 * 512 + 512],
                                        in_=bank[:])
                                evn = P.emit("vector", fe,
                                             waits=[tk, natst_guard[0]],
                                             inc="v")
                                last_touch[("r", rpar)] = evn
                                evns.append(evn)

                            def fdo(e, tT=tT, lo=lo):
                                r0 = (lo - 2) + tT * 128
                                return e.dma_start(
                                    out=y_out[r0:r0 + 128, :],
                                    in_=natst[:])
                            natst_guard[0] = P.emit("sync", fdo, waits=evns,
                                                    inc="dout")

                prep(0)
                for isc in range(len(out_chunks)):
                    if isc + 1 < len(out_chunks):
                        prep(isc + 1)
                    compute(isc)
        def xr_wait_p1(lo, hi):
            toks = []
            for g in range(4):
                if lo < (g + 1) * 512 and hi > g * 512:
                    toks.extend(xr_ready_g[g])
            if hi > 2048:
                toks.extend(xr_tail)
            return toks

        ffn1_chunks = [(1, 513), (513, 1025), (1025, 1537), (1537, 2049),
                       (2049, 2050)]
        ffn1_out = {}
        ffn((0, TL), ffn1_chunks, False, xr_wait_p1, p0_join, wt_ready_ffn1,
            ffn1_out, "1")

        pk_guard = None
        pk_i = 0
        for wi, src_h in ((0, Wk_h), (1, Wv_h), (2, Wq_h)):
            for q4 in range(4):
                thr = ffn1_out[min(1 + pk_i // 6, 2)][-1]
                pk_i += 1

                def fl(e, src_h=src_h, q4=q4):
                    return e.dma_start(
                        out=park[:].rearrange("p (dt c) -> p dt c", dt=2),
                        in_=bass.AP(src_h.tensor, q4 * 2 * 128 * D,
                                    [[D, 128], [128 * D, 2], [1, D]]))
                ldp = P.emit("gpsimd", fl, waits=[pk_guard, thr], inc="dpk0")

                def fs(e, wi=wi, q4=q4):
                    return e.dma_start(
                        out=wattn.ap()[wi * 128:(wi + 1) * 128,
                                       q4 * 2 * D:(q4 + 1) * 2 * D],
                        in_=park[:])
                pk_guard = P.emit("sync", fs, waits=[ldp], inc="dpk1")
        attn_sc_done = [("dpk0", P.cnt["dpk0"]), ("dpk1", P.cnt["dpk1"])]

        b1 = P.barrier()
        if KDBG:
            P.emit("sync", lambda e: e.dma_start(out=dbg_f1, in_=XR[:, 0:CW]),
                   waits=b1, inc="ddbg")

        ast = contextlib.ExitStack()
        with ast:
            inv_row2 = ast.enter_context(nc.sbuf_tensor("a_inv", [1, CW], F32))
            negminvb = ast.enter_context(nc.sbuf_tensor("a_negmb", [1, CW], BF16))
            inv128bc = ast.enter_context(nc.sbuf_tensor("a_inv128", [128, CW], F32))
            colsk = ast.enter_context(nc.sbuf_tensor("a_colsk", [1, D], BF16))
            colsv = ast.enter_context(nc.sbuf_tensor("a_colsv", [1, D], BF16))
            bk_row = ast.enter_context(nc.sbuf_tensor("bkrow", [1, D], BF16))
            bv_row = ast.enter_context(nc.sbuf_tensor("bvrow", [1, D], BF16))
            ps_a0 = ast.enter_context(nc.psum_tensor("a_pa0", [128, 512], F32))
            ps_a1 = ast.enter_context(nc.psum_tensor("a_pa1", [128, 512], F32))
            ps_b0 = ast.enter_context(nc.psum_tensor("a_pb0", [128, 512], F32))
            ps_b1 = ast.enter_context(nc.psum_tensor("a_pb1", [128, 512], F32))
            ps_c0 = ast.enter_context(nc.psum_tensor("a_pc0", [128, 512], F32))
            ps_c1 = ast.enter_context(nc.psum_tensor("a_pc1", [128, 512], F32))
            ps_d0 = ast.enter_context(nc.psum_tensor("a_pd0", [128, 512], F32))
            ps_d1 = ast.enter_context(nc.psum_tensor("a_pd1", [128, 512], F32))
            c_bk = P.emit("sync", lambda e: e.dma_start(
                out=bk_row[:], in_=brow_sc.ap()[0:1, :]), waits=b1 + [brow_done],
                inc="dbias")
            c_bv = P.emit("sync", lambda e: e.dma_start(
                out=bv_row[:], in_=brow_sc.ap()[1:2, :]), waits=[c_bk],
                inc="dbias")
            bias_all = ("dbias", P.cnt["dbias"])
            c_bk = c_bv = bias_all

            def xr_wait_p2(lo, hi):
                toks = list(b1)
                for i, (clo, chi) in enumerate(ffn1_chunks):
                    if lo < chi and hi > clo:
                        toks.extend(ffn1_out[i])
                return toks

            s2st = contextlib.ExitStack()
            with s2st:
                cbuf2 = s2st.enter_context(nc.sbuf_tensor("a_cb", [128, 4 * 512], BF16))
                sqtmp2 = s2st.enter_context(nc.sbuf_tensor("a_sq", [128, 8 * 256], BF16))
                rs1b = s2st.enter_context(nc.sbuf_tensor("a_rs1", [1, 512], F32))
                rs2b = s2st.enter_context(nc.sbuf_tensor("a_rs2", [1, 512], F32))
                rs3b = s2st.enter_context(nc.sbuf_tensor("a_rs3", [1, 512], F32))
                scope2 = (cbuf2, sqtmp2, rs1b, rs2b, rs3b, ps_a0)
                _c2, _e2, st2 = make_stats(scope2, 1, TL, inv_row2,
                                           negminv_bf=negminvb,
                                           wait_cols=xr_wait_p2, entry=b1,
                                           bc_dst=inv128bc, bc_psum=ps_a1)
                for _ci in range(len(_c2)):
                    _e2(_ci)
            st2t = [t for (_a, _b, t) in st2]

            XBF = ast.enter_context(nc.sbuf_tensor("a_XBF", [128, NT * 516], BF16))
            kvst = contextlib.ExitStack()
            K8 = kvst.enter_context(nc.sbuf_tensor("a_K8", [128, 4 * D], BF16))
            V8 = kvst.enter_context(nc.sbuf_tensor("a_V8", [128, 4 * D], BF16))
            C_sb = kvst.enter_context(nc.sbuf_tensor("a_Csb", [128, NT * D], F32))

            bc_chunks = _chunks(1, TL)
            bc_done_inv = st2t[-1]

            spill_chs = ["dspill", "dsp2", "dsp3"]
            load_chs = ["dwA", "dwA2", "dwA3"]
            spill_toks = []
            for i in range(3):
                def fsp(e, i=i):
                    return e.dma_start(
                        out=wspill.ap()[i * 128:(i + 1) * 128, :], in_=WB[i][:])
                spill_toks.append(P.emit("sync", fsp, waits=b1,
                                         inc=spill_chs[i]))
            spill_all = list(spill_toks)
            attn_w = []
            for i in range(3):
                def fl(e, i=i):
                    return e.dma_start(
                        out=WB[i][:],
                        in_=wattn.ap()[i * 128:(i + 1) * 128, :])
                attn_w.append(P.emit("sync", fl,
                                     waits=[spill_toks[i], attn_sc_done],
                                     inc=load_chs[i]))

            cs_prev = None
            for (dst, wi) in ((colsk, 0), (colsv, 1)):
                for nh in range(2):
                    for dT in range(8):
                        def fm(e, wi=wi, dT=dT, nh=nh):
                            return e.matmul(
                                out=ps_b0[0:1, 0:512], lhsT=ones_col_b[:],
                                rhs=WB[wi][:, dT * D + nh * 512:
                                           dT * D + nh * 512 + 512],
                                start=(dT == 0), stop=(dT == 7))
                        tk = P.emit("tensor", fm,
                                    waits=[attn_w[wi], t_consts, cs_prev],
                                    inc="t" if dT == 7 else None)
                    cs_prev = P.emit("scalar",
                                     lambda e, dst=dst, nh=nh: e.copy(
                                         out=dst[0:1, nh * 512:nh * 512 + 512],
                                         in_=ps_b0[0:1, 0:512]),
                                     waits=[tk], inc="s")
            cs_done = cs_prev

            kv_banks = [ps_a0, ps_a1, ps_b0, ps_b1]
            kv_bank_guard = {0: bc_done_inv, 1: st2t[-1], 2: cs_done, 3: cs_done}
            C_banks = {(0, 0): ps_c0, (0, 1): ps_c1, (1, 0): ps_d0,
                       (1, 1): ps_d1}
            c_bank_guard = {}
            c_ev_mc = {}
            stg_guard = {}
            kv_last = {}
            cdma_prev = None
            bank_i = 0
            for q in range(4):
                wlo = 1 + 512 * q
                xb_toks = []
                for dT in range(8):
                    def fx(e, dT=dT, wlo=wlo):
                        return e.tensor_mul(
                            out=XBF[:, dT * 516:dT * 516 + 513],
                            in0=XR[:, dT * CW + wlo: dT * CW + wlo + 513],
                            in1=inv128bc[:, wlo:wlo + 513])
                    wz = xr_wait_p2(wlo, wlo + 513) + \
                        _cover(st2, wlo, wlo + 513) + [kv_last.get(q - 1)]
                    xb_toks.append(P.emit("vector", fx, waits=wz,
                                          inc="v" if dT == 7 else None))
                xb_tok = xb_toks[-1]
                ev_kv = {}
                for tt in range(4):
                    gcol = 2 + 512 * q + 128 * tt
                    lcol = gcol - wlo
                    for (kv_i, wslot, colrow, cbias) in ((0, 0, colsk, c_bk),
                                                         (1, 1, colsv, c_bv)):
                        for nh in range(2):
                            bank = kv_banks[bank_i % 4]
                            this_bank = bank_i % 4
                            bank_i += 1
                            for dT in range(8):
                                def fm(e, dT=dT, nh=nh, wslot=wslot,
                                       lcol=lcol, bank=bank):
                                    return e.matmul(
                                        out=bank[:],
                                        lhsT=XBF[:, dT * 516 + lcol:
                                                 dT * 516 + lcol + 128],
                                        rhs=WB[wslot][:, dT * D + nh * 512:
                                                      dT * D + nh * 512 + 512],
                                        start=(dT == 0), stop=False)
                                w = ([xb_tok, attn_w[wslot],
                                      kv_bank_guard.get(this_bank),
                                      cbias, cs_done]
                                     + _cover(st2, gcol, gcol + 128)
                                     ) if dT == 0 else []
                                P.emit("tensor", fm, waits=w)
                            brow = bk_row if kv_i == 0 else bv_row

                            def fb1(e, nh=nh, brow=brow, bank=bank):
                                return e.matmul(
                                    out=bank[:], lhsT=ones_row_b[:],
                                    rhs=brow[0:1, nh * 512:nh * 512 + 512],
                                    start=False, stop=False)
                            P.emit("tensor", fb1)

                            def fb2(e, nh=nh, colrow=colrow, gcol=gcol,
                                    bank=bank):
                                return e.matmul(
                                    out=bank[:],
                                    lhsT=negminvb[0:1, gcol:gcol + 128],
                                    rhs=colrow[0:1, nh * 512:nh * 512 + 512],
                                    start=False, stop=True)
                            tk = P.emit("tensor", fb2, inc="t")
                            if kv_i == 0:
                                def fe(e, tt=tt, nh=nh, bank=bank):
                                    return e.copy(
                                        out=K8[:, tt * D + nh * 512:
                                               tt * D + nh * 512 + 512],
                                        in_=bank[:])
                                ev = P.emit("scalar", fe, waits=[tk], inc="s")
                            else:
                                def fe(e, tt=tt, nh=nh, bank=bank):
                                    return e.tensor_copy(
                                        out=V8[:, tt * D + nh * 512:
                                               tt * D + nh * 512 + 512],
                                        in_=bank[:])
                                ev = P.emit("vector", fe, waits=[tk], inc="v")
                            kv_bank_guard[this_bank] = ev
                            ev_kv[(kv_i, tt, nh)] = ev
                kv_done = [ev_kv[k] for k in ev_kv]
                kv_last[q] = ev_kv[(1, 3, 1)]
                for mc in range(8):
                    for nh in range(2):
                        bank = C_banks[(mc % 2, nh)]
                        for tt in range(4):
                            def fm(e, mc=mc, nh=nh, tt=tt, bank=bank):
                                return e.matmul(
                                    out=bank[:],
                                    lhsT=K8[:, tt * D + mc * 128:
                                            tt * D + mc * 128 + 128],
                                    rhs=V8[:, tt * D + nh * 512:
                                           tt * D + nh * 512 + 512],
                                    start=(tt == 0), stop=(tt == 3))
                            w = kv_done + [c_bank_guard.get((mc % 2, nh))]
                            tk = P.emit("tensor", fm, waits=w,
                                        inc="t" if tt == 3 else None)

                        def fe(e, bank=bank, mc=mc, nh=nh, q=q):
                            dst = C_sb[:, mc * D + nh * 512:
                                       mc * D + nh * 512 + 512]
                            if q == 0:
                                return e.tensor_copy(out=dst, in_=bank[:])
                            return e.tensor_add(out=dst, in0=dst, in1=bank[:])
                        ev = P.emit("vector", fe, waits=[tk], inc="v")
                        c_bank_guard[(mc % 2, nh)] = ev
                        c_ev_mc[(mc, nh)] = ev
                        cdma_prev = ev

            cw_tok = {}
            for mc in range(8):
                for rep in range(2):
                    def fw(e, mc=mc, rep=rep):
                        return e.dma_start(
                            out=bass.AP(Cdd, mc * 128 * 2048 + rep * 1024,
                                        [[2048, 128], [1, 1024]]),
                            in_=C_sb[:, mc * D:(mc + 1) * D])
                    cw_tok[mc] = P.emit("gpsimd", fw,
                                        waits=[c_ev_mc[(mc, 0)],
                                               c_ev_mc[(mc, 1)]],
                                        inc="dC%d" % mc)
            dup = cw_tok[7]
            kvst.close()
            rot_all = ast.enter_context(nc.sbuf_tensor("a_rot", [128, 4 * D], BF16))
            minv128bc = ast.enter_context(nc.sbuf_tensor("a_minv128", [128, CW], F32))
            srow = ast.enter_context(nc.sbuf_tensor("a_srow", [1, D], F32))
            srowb = ast.enter_context(nc.sbuf_tensor("a_srowb", [1, D], BF16))
            bq_colt = ast.enter_context(nc.sbuf_tensor("a_bqc", [128, 8], BF16))
            colq_c = ast.enter_context(nc.sbuf_tensor("a_colq", [128, 8], F32))
            bqS_c = ast.enter_context(nc.sbuf_tensor("a_bqS", [128, 8], F32))
            tmpa = [ast.enter_context(nc.sbuf_tensor(
                "a_tmpa%d" % i, [128, 512], F32)) for i in range(2)]
            bq_row = ast.enter_context(nc.sbuf_tensor("bqrow", [1, D], BF16))
            c_bq = P.emit("sync", lambda e: e.dma_start(
                out=bq_row[:], in_=brow_sc.ap()[2:3, :]),
                waits=[bias_all, kv_last[3], cdma_prev]
                + list(cw_tok.values()), inc="dbias")

            wq_banks = [ps_c0[:].bitcast(BF16), ps_c1[:].bitcast(BF16)]
            prevq = {0: c_bank_guard.get((0, 0)), 1: c_bank_guard.get((1, 0))}
            wq_entry = [cdma_prev, kv_last[3], attn_w[2], t_idb]
            wqT_done = None
            for eT in range(8):
                for half in range(2):
                    par = (eT * 2 + half) % 2
                    bank = wq_banks[par]
                    for j in range(4):
                        dT = half * 4 + j

                        def ft(e, eT=eT, dT=dT, j=j, bank=bank):
                            return e.transpose(
                                out=bank[:, j * 128:(j + 1) * 128],
                                in_=WB[2][:, dT * D + eT * 128:
                                          dT * D + eT * 128 + 128],
                                identity=identb[:])
                        w = wq_entry + [prevq.get(par)]
                        tk = P.emit("tensor", ft, waits=w,
                                    inc="t" if j == 3 else None)

                    def fe(e, eT=eT, half=half, bank=bank):
                        return e.copy(
                            out=WB[0][:, eT * D + half * 512:
                                      eT * D + half * 512 + 512],
                            in_=bank[:, 0:512])
                    prevq[par] = P.emit("scalar", fe, waits=[tk], inc="s")
                    wqT_done = prevq[par]

            ev_s0 = ev_s1 = None
            s_mm_prev = None
            for rnd in range(2):
                rd = None
                for m4 in range(4):
                    mT = rnd * 4 + m4

                    def fr(e, mT=mT, m4=m4):
                        src = bass.AP(Cdd, mT * 128 * 2047 + 1024,
                                      [[2047, 128], [1, 1024]])
                        return e.dma_start(
                            out=rot_all[:, m4 * D:(m4 + 1) * D], in_=src)
                    rd = P.emit("sync", fr,
                                waits=[cw_tok[mT], s_mm_prev],
                                inc="drot%d" % rnd)
                s_mm = None
                for nh in range(2):
                    for m4 in range(4):
                        def fm(e, m4=m4, nh=nh):
                            bank = ps_a0 if nh == 0 else ps_a1
                            return e.matmul(
                                out=bank[0:1, 0:512], lhsT=ones_col_b[:],
                                rhs=rot_all[:, m4 * D + nh * 512:
                                            m4 * D + nh * 512 + 512],
                                start=(m4 == 0), stop=(m4 == 3))
                        w = ([rd, t_consts, kv_bank_guard.get(0),
                              kv_bank_guard.get(1), ev_s0, ev_s1]
                             if m4 == 0 else [])
                        s_mm = P.emit("tensor", fm, waits=w,
                                      inc="t" if m4 == 3 else None)
                s_mm_prev = s_mm
                if rnd == 0:
                    ev_s0 = P.emit("vector", lambda e: e.tensor_copy(
                        out=srow[0:1, 0:512], in_=ps_a0[0:1, 0:512]),
                        waits=[s_mm], inc="v")
                    ev_s1 = P.emit("vector", lambda e: e.tensor_copy(
                        out=srow[0:1, 512:1024], in_=ps_a1[0:1, 0:512]),
                        waits=[s_mm], inc="v")
                else:
                    ev_s0 = P.emit("vector", lambda e: e.tensor_add(
                        out=srow[0:1, 0:512], in0=srow[0:1, 0:512],
                        in1=ps_a0[0:1, 0:512]), waits=[s_mm, ev_s0], inc="v")
                    ev_s1 = P.emit("vector", lambda e: e.tensor_add(
                        out=srow[0:1, 512:1024], in0=srow[0:1, 512:1024],
                        in1=ps_a1[0:1, 0:512]), waits=[s_mm, ev_s1], inc="v")
            cc_stage = ast.enter_context(nc.sbuf_tensor("a_ccst", [128, 128], F32))
            z_cc = P.emit("vector", lambda e: e.memset(cc_stage[:], 0.0),
                          waits=[kv_last[3], cdma_prev]
                          + list(cw_tok.values()), inc="v")
            d_s0 = P.emit("gpsimd", lambda e: e.dma_start(
                out=cc_stage[0:8, :], in_=srow[:]),
                waits=[ev_s0, ev_s1, z_cc], inc="dgs")
            d_s = P.emit("gpsimd", lambda e: e.dma_start(
                out=cc_in.ap(), in_=cc_stage[:]), waits=[d_s0],
                inc="dgs")
            import os as _os
            if _os.environ.get("NO_CC"):
                t_cc = P.emit("gpsimd", lambda e: e.dma_start(
                    out=cc_out.ap(), in_=cc_in.ap()), waits=[d_s], inc="cc")
            else:
                t_cc = P.emit("gpsimd", lambda e: e.collective_compute(
                    "AllReduce", OP.add,
                    replica_groups=[[0, 1], [2, 3], [4, 5], [6, 7]],
                    ins=[cc_in.ap().opt()], outs=[cc_out.ap().opt()]),
                    waits=[d_s], inc="cc")
            d_s2 = P.emit("gpsimd", lambda e: e.dma_start(
                out=srow[:], in_=cc_out.ap()[0:8, :]),
                waits=[t_cc, ev_s0, ev_s1], inc="dgs")
            ev_sb = P.emit("vector", lambda e: e.tensor_copy(
                out=srowb[:], in_=srow[:]), waits=[d_s2], inc="v")
            prev_mv = [ev_s0, ev_s1]
            for ci, (lo, hi) in enumerate(bc_chunks):
                W = hi - lo

                def fb(e, lo=lo, W=W):
                    return e.matmul(out=ps_a0[:, 0:W], lhsT=negones_row_b[:],
                                    rhs=negminvb[0:1, lo:lo + W],
                                    start=True, stop=True)
                tb = P.emit("tensor", fb,
                            waits=_cover(st2, lo, hi) + prev_mv, inc="t")
                prev_mv = [P.emit("vector",
                                  lambda e, lo=lo, W=W: e.tensor_copy(
                                      out=minv128bc[:, lo:lo + W],
                                      in_=ps_a0[:, 0:W]), waits=[tb], inc="v")]
            bc_done = prev_mv[0]
            d_sd0 = P.emit("sync", lambda e: e.dma_start(
                out=sdd.ap()[0:1, 0:1024], in_=srowb[:]), waits=[ev_sb],
                inc="dsdd")
            d_sd1 = P.emit("sync", lambda e: e.dma_start(
                out=sdd.ap()[0:1, 1024:2048], in_=srowb[:]), waits=[ev_sb],
                inc="dsdd")

            s_tok = {}
            for eT in range(8):
                def fS(e, eT=eT):
                    src = bass.AP(sdd, eT * 128, [[1, 128], [1, 1024]])
                    return e.dma_start(out=WB[1][:, eT * D:(eT + 1) * D],
                                       in_=src)
                s_tok[eT % 2] = P.emit("sync", fS,
                                       waits=[d_sd0, d_sd1, kv_last[3]],
                                       inc="dS" if eT % 2 == 0 else "drot1")
            s_ready = [s_tok[0], s_tok[1]]

            qp_banks = [ps_b0, ps_b1]
            prevp = {0: kv_bank_guard[2], 1: kv_bank_guard[3]}
            wqp_done = None
            for dC in range(8):
                for nh in range(2):
                    par = (dC * 2 + nh) % 2
                    bank = qp_banks[par]
                    for eT in range(8):
                        def fm(e, dC=dC, nh=nh, eT=eT, bank=bank):
                            return e.matmul(
                                out=bank[:],
                                lhsT=WB[0][:, eT * D + dC * 128:
                                           eT * D + dC * 128 + 128],
                                rhs=WB[1][:, eT * D + nh * 512:
                                          eT * D + nh * 512 + 512],
                                start=(eT == 0), stop=(eT == 7))
                        w = [wqT_done, s_ready, prevp.get(par)]
                        tk = P.emit("tensor", fm, waits=w,
                                    inc="t" if eT == 7 else None)

                    def fe(e, dC=dC, nh=nh, bank=bank):
                        return e.copy(
                            out=WB[2][:, dC * D + nh * 512:
                                      dC * D + nh * 512 + 512],
                            in_=bank[:])
                    prevp[par] = P.emit("scalar", fe, waits=[tk], inc="s")
                    wqp_done = prevp[par]

            bq_bank = ps_c0[:].bitcast(BF16)
            for j in range(8):
                def ft(e, j=j):
                    return e.transpose(out=bq_bank[:, 2 * j:2 * j + 1],
                                       in_=bq_row[0:1, j * 128:j * 128 + 128],
                                       identity=identb[0:1, 0:1])
                tbq = P.emit("tensor", ft, waits=[c_bq, t_idb, wqT_done],
                             inc="t" if j == 7 else None)
            ev_bqc = P.emit("vector", lambda e: e.tensor_copy(
                out=bq_colt[:], in_=bq_bank[:, 0:16:2]), waits=[tbq], inc="v")
            for nC in range(8):
                for dT in range(8):
                    def fm(e, nC=nC, dT=dT):
                        return e.matmul(
                            out=ps_c1[:, nC:nC + 1],
                            lhsT=WB[2][:, dT * D + nC * 128:
                                       dT * D + nC * 128 + 128],
                            rhs=ones_col_b[:],
                            start=(dT == 0), stop=(dT == 7))
                    tcq = P.emit("tensor", fm,
                                 waits=[wqp_done, t_consts,
                                        c_bank_guard.get((1, 0))],
                                 inc="t" if (nC == 7 and dT == 7) else None)
                for eT in range(8):
                    def fm2(e, nC=nC, eT=eT):
                        return e.matmul(
                            out=ps_d0[:, nC:nC + 1],
                            lhsT=WB[1][:, eT * D + nC * 128:
                                       eT * D + nC * 128 + 128],
                            rhs=bq_colt[:, eT:eT + 1],
                            start=(eT == 0), stop=(eT == 7))
                    tbs = P.emit("tensor", fm2,
                                 waits=[ev_bqc, s_ready,
                                        c_bank_guard.get((1, 0))],
                                 inc="t" if (nC == 7 and eT == 7) else None)
            ev_cq = P.emit("scalar", lambda e: e.mul(
                out=colq_c[:], in_=ps_c1[:, 0:8], mul=-1.0),
                waits=[tcq, t_consts], inc="s")
            ev_bqs = P.emit("vector", lambda e: e.tensor_copy(
                out=bqS_c[:], in_=ps_d0[:, 0:8]), waits=[tbs], inc="v")

            if KDBG:
                for wi3 in range(3):
                    P.emit("sync", lambda e, wi3=wi3: e.dma_start(
                        out=dbg_wq[wi3 * 128:(wi3 + 1) * 128, :],
                        in_=WB[wi3][:]),
                        waits=[wqp_done, s_ready, wqT_done, ev_bqs, ev_cq],
                        inc="ddbg")
                P.emit("sync", lambda e: e.dma_start(
                    out=dbg_bc[0:128, :], in_=inv128bc[:]),
                    waits=[bc_done_inv], inc="ddbg")
                P.emit("sync", lambda e: e.dma_start(
                    out=dbg_bc[128:256, :], in_=minv128bc[:]),
                    waits=[bc_done], inc="ddbg")
                P.emit("sync", lambda e: e.dma_start(
                    out=dbg_cq[:, 0:8], in_=colq_c[:]),
                    waits=[ev_cq], inc="ddbg")
                P.emit("sync", lambda e: e.dma_start(
                    out=dbg_cq[:, 8:16], in_=bqS_c[:]),
                    waits=[ev_bqs], inc="ddbg")
                P.emit("sync", lambda e: e.dma_start(
                    out=dbg_cq[:, 16:24].bitcast(mybir.dt.bfloat16)[:, 0:8],
                    in_=bq_colt[:]),
                    waits=[ev_bqc], inc="ddbg")

            ao_chunks = [(1, 513), (513, 1025), (1025, 1537), (1537, 2049),
                         (2049, 2050)]
            ao_done = {}
            ao_bank_guard = {0: c_bank_guard.get((0, 1)),
                             1: c_bank_guard.get((1, 1))}
            tmpa_guard = {}
            ao_prev = None
            for j, (lo, hi) in enumerate(ao_chunks):
                W = hi - lo
                xb_toks = []
                for dT in range(8):
                    def fx(e, dT=dT, lo=lo, W=W):
                        return e.tensor_mul(
                            out=XBF[:, dT * 516:dT * 516 + W],
                            in0=XR[:, dT * CW + lo: dT * CW + lo + W],
                            in1=inv128bc[:, lo:lo + W])
                    wz = [bc_done, kv_last[3], ao_prev]
                    xb_toks.append(P.emit("vector", fx, waits=wz,
                                          inc="v" if dT == 7 else None))
                xb_tok = xb_toks[-1]
                outs = []
                for nC in range(8):
                    par = nC % 2
                    bank = [ps_a0, ps_a1][par]
                    for dT in range(8):
                        def fm(e, nC=nC, dT=dT, W=W, bank=bank):
                            return e.matmul(
                                out=bank[:, 0:W],
                                lhsT=WB[2][:, dT * D + nC * 128:
                                           dT * D + nC * 128 + 128],
                                rhs=XBF[:, dT * 516:dT * 516 + W],
                                start=(dT == 0), stop=(dT == 7))
                        w = [xb_tok, wqp_done, ao_bank_guard.get(par),
                             ev_s0, ev_s1]
                        tk = P.emit("tensor", fm, waits=w,
                                    inc="t" if dT == 7 else None)

                    def fe1(e, nC=nC, lo=lo, W=W, bank=bank):
                        return e.scalar_tensor_tensor(
                            out=tmpa[nC % 2][:, 0:W],
                            in0=minv128bc[:, lo:lo + W],
                            scalar=colq_c[:, nC:nC + 1], in1=bank[:, 0:W],
                            op0=OP.mult, op1=OP.add)
                    e1 = P.emit("vector", fe1,
                                waits=[tk, ev_cq, bc_done,
                                       tmpa_guard.get(nC % 2)],
                                inc="v")
                    ao_bank_guard[par] = e1

                    def fe2(e, nC=nC, lo=lo, W=W):
                        return e.scalar_tensor_tensor(
                            out=XR[:, nC * CW + lo: nC * CW + lo + W],
                            in0=tmpa[nC % 2][:, 0:W],
                            scalar=bqS_c[:, nC:nC + 1],
                            in1=XR[:, nC * CW + lo: nC * CW + lo + W],
                            op0=OP.add, op1=OP.add)
                    e2 = P.emit("vector", fe2, waits=[e1, ev_bqs], inc="v")
                    tmpa_guard[nC % 2] = e2
                    outs.append(e2)
                ao_done[j] = outs
                ao_prev = outs[-1]

            reload_tok = {}
            reload_last = None
            free_tok = {0: wqp_done, 1: ev_bqs, 2: ao_done[4][-1]}
            for i in range(3):
                def frl(e, i=i):
                    return e.dma_start(
                        out=WB[i][:],
                        in_=wspill.ap()[i * 128:(i + 1) * 128, :])
                reload_last = P.emit("sync", frl,
                                     waits=[free_tok[i], spill_all],
                                     inc="drl")
                reload_tok[i] = reload_last
            if KDBG:
                P.emit("sync", lambda e: e.dma_start(out=dbg_at, in_=XR[:, 0:CW]),
                       waits=[ao_done[j][-1] for j in range(5)], inc="ddbg")
                P.emit("sync", lambda e: e.dma_start(out=dbg_s, in_=srow[:]),
                       waits=[ev_sb], inc="ddbg")
            b2 = P.barrier()

        ao_chunks_p4 = [(1, 513), (513, 1025), (1025, 1537), (1537, 2049),
                        (2049, 2050)]

        def xr_wait_p4(lo, hi):
            toks = []
            for j, (clo, chi) in enumerate(ao_chunks_p4):
                if lo < chi and hi > clo:
                    toks.extend(ao_done[j])
            return toks

        ffn2_chunks = [(2, 514), (514, 1026), (1026, 1538), (1538, 2050)]
        ffn2_out = {}
        ffn((1, TL), ffn2_chunks, True, xr_wait_p4, b2 + [reload_last],
            {"r": reload_tok[0], "k": reload_tok[1], "w": reload_tok[2]},
            ffn2_out, "2")

        fin = [(k, v) for k, v in P.cnt.items()
               if k.startswith("d") and v > 0]
        P.emit("sync", lambda e: e.nop(), waits=fin)
        P.emit("vector", lambda e: e.nop(), waits=fin)

        with nc.Block() as block:
            P.flush(block)

    return nc


_NC = None


def kernel(L0_attn_x, L0_attn_Wq, L0_attn_bq, L0_attn_Wk, L0_attn_bk,
           L0_attn_Wv, L0_attn_bv, L0_ffn_W_r, L0_ffn_W_k, L0_ffn_W_v,
           L0_ffn_mu_r, L0_ffn_mu_k):
    global _NC
    if _NC is None:
        _NC = build_nc()
    x = np.ascontiguousarray(np.asarray(L0_attn_x, np.float32))
    B, T, Dd = x.shape
    pad = np.zeros((2, Dd), np.float32)
    common = {
        "Wq": np.ascontiguousarray(np.asarray(L0_attn_Wq, np.float32)),
        "Wk": np.ascontiguousarray(np.asarray(L0_attn_Wk, np.float32)),
        "Wv": np.ascontiguousarray(np.asarray(L0_attn_Wv, np.float32)),
        "bq": np.ascontiguousarray(np.asarray(L0_attn_bq, np.float32).reshape(1, -1)),
        "bk": np.ascontiguousarray(np.asarray(L0_attn_bk, np.float32).reshape(1, -1)),
        "bv": np.ascontiguousarray(np.asarray(L0_attn_bv, np.float32).reshape(1, -1)),
        "Wr": np.ascontiguousarray(np.asarray(L0_ffn_W_r, np.float32)),
        "Wkf": np.ascontiguousarray(np.asarray(L0_ffn_W_k, np.float32)),
        "Wvf": np.ascontiguousarray(np.asarray(L0_ffn_W_v, np.float32)),
        "mur": np.ascontiguousarray(np.asarray(L0_ffn_mu_r, np.float32).reshape(1, -1)),
        "muk": np.ascontiguousarray(np.asarray(L0_ffn_mu_k, np.float32).reshape(1, -1)),
    }
    in_maps = []
    for c in range(8):
        b, h = c // 2, c % 2
        if h == 0:
            shard = np.concatenate([pad, x[b, 0:2048]], axis=0)
        else:
            shard = np.ascontiguousarray(x[b, 2046:4096])
        d = dict(common)
        d["xs"] = shard
        d["maskc"] = np.full((128, 1), float(h), np.float32)
        d["identin"] = np.eye(128, dtype=np.float32)
        in_maps.append(d)
    res = run_bass_kernel_spmd(_NC, in_maps, core_ids=list(range(8)))
    kernel.last = res.results
    out = np.empty((B, T, Dd), np.float32)
    for c in range(8):
        b, h = c // 2, c % 2
        out[b, h * 2048:(h + 1) * 2048] = res.results[c]["y"]
    return out
